# version 14
# baseline (speedup 1.0000x reference)
import math
from contextlib import ExitStack

import numpy as np
import ml_dtypes

import concourse.bass as bass
import concourse.mybir as mybir
from concourse.bass_utils import run_bass_kernel_spmd

F32 = mybir.dt.float32
BF16 = mybir.dt.bfloat16
U8 = mybir.dt.uint8
AF = mybir.ActivationFunctionType
ALU = mybir.AluOpType
AX = mybir.AxisListType
bfdt = ml_dtypes.bfloat16

D = 2048
SEQ = 2048
NBATCH = 16
NH = 16
DH = 128
INTER = 8192
N_CTRL = 130156
NCORES = 8
BPC = NBATCH // NCORES
TOK = BPC * SEQ
NCH = D // 128
NIC = INTER // 128
EPS = 1e-5
SCALE = 1.0 / math.sqrt(DH)
NIB = SEQ // 128
NSTRIP = SEQ // 512

NSAMP = INTER * D
DT64 = (N_CTRL - 1.0) / (NSAMP - 1.0)

CK = np.array([
    [0.0, -0.5, 1.0, -0.5],
    [1.0, 0.0, -2.5, 1.5],
    [0.0, 0.5, 2.0, -1.5],
    [0.0, 0.0, -0.5, 0.5],
], dtype=np.float64)


def _build_B80():
    p = np.arange(128, dtype=np.float64)
    out = []
    for beta in (0, 1):
        B = np.zeros((128, 20), dtype=np.float64)
        a = p * DT64 - beta
        for k in range(4):
            al = CK[k]
            g = [al[0] + al[1] * a + al[2] * a * a + al[3] * a * a * a,
                 al[1] + 2 * al[2] * a + 3 * al[3] * a * a,
                 al[2] + 3 * al[3] * a,
                 np.full(128, al[3])]
            m = k + beta
            for j in range(4):
                B[:, j * 5 + m] += g[j]
        Bh = B.astype(bfdt).astype(np.float64)
        Bl = (B - Bh).astype(np.float32).astype(bfdt).astype(np.float64)
        B80 = np.concatenate([Bh, Bh, Bl, Bl], axis=1)
        out.append(np.ascontiguousarray(B80.T).astype(np.float32).astype(bfdt))
    return np.stack(out)


def _recon_consts(out_size, in_size):
    nC = in_size // 128
    n = out_size * in_size
    t = np.linspace(0.0, N_CTRL - 1.0, n, dtype=np.float64)
    i_arr = np.clip(np.floor(t).astype(np.int64), 0, N_CTRL - 2)
    o = np.arange(out_size, dtype=np.int64)
    C = np.arange(nC, dtype=np.int64)
    n0 = o[None, :] * in_size + 128 * C[:, None]
    S = t[n0]
    I0 = i_arr[n0]
    u = S - I0
    upow = np.stack([np.ones_like(u), u, u * u, u * u * u], axis=-1).astype(np.float32)
    idx128 = n0[:, :, None] + np.arange(128)[None, None, :]
    s = (i_arr[idx128] - I0[:, :, None]).astype(np.uint8)
    tapidx = np.clip(I0[:, :, None] + np.arange(-1, 4)[None, None, :], 0, N_CTRL - 1)
    n_oc = out_size // 128
    tapidx = tapidx.transpose(1, 0, 2).reshape(n_oc, 128, nC, 5)
    upow = upow.transpose(1, 0, 2).reshape(n_oc, 128, nC, 4)
    mask = np.ascontiguousarray(s.transpose(0, 2, 1))
    return tapidx, upow, mask


_RECON_CACHE = {}


def _get_recon(out_size, in_size):
    key = (out_size, in_size)
    if key not in _RECON_CACHE:
        _RECON_CACHE[key] = _recon_consts(out_size, in_size)
    return _RECON_CACHE[key]


class Prog:
    def __init__(self, nc, es):
        self.nc = nc
        self.es = es
        self.sems = {}

    def sem(self, name):
        s = self.es.enter_context(self.nc.semaphore(name))
        self.sems[name] = [s, 0]

    def inc(self, inst, name, d=1):
        s = self.sems[name]
        inst.then_inc(s[0], d)
        s[1] += d
        return s[1]

    def count(self, name):
        return self.sems[name][1]

    def wait(self, eng, name, n=None):
        s = self.sems[name]
        n = s[1] if n is None else n
        if n > 0:
            eng.wait_ge(s[0], n)


def build_program(dbg=False):
    nc = bass.Bass("TRN2", target_bir_lowering=False)
    KIND = "ExternalOutput" if dbg else "Internal"

    xT = nc.dram_tensor("xT", [NCH, 128, TOK], F32, kind="ExternalInput").ap()
    wqT = nc.dram_tensor("wqT", [NH, 128, NCH, DH], F32, kind="ExternalInput").ap()
    wkT = nc.dram_tensor("wkT", [NH, 128, NCH, DH], F32, kind="ExternalInput").ap()
    wvT = nc.dram_tensor("wvT", [NH, 128, NCH, DH], F32, kind="ExternalInput").ap()
    woT = nc.dram_tensor("woT", [NCH, 128, NCH, 128], F32, kind="ExternalInput").ap()
    rin = {}
    for w, (O, I) in (("g", (INTER, D)), ("u", (INTER, D)), ("d", (D, INTER))):
        nC = I // 128
        n_oc = O // 128
        rin[w] = dict(
            taps=nc.dram_tensor(f"taps_{w}", [n_oc, 128, nC, 8], F32,
                                kind="ExternalInput").ap(),
            upow=nc.dram_tensor(f"upow_{w}", [n_oc, 128, nC, 4], F32,
                                kind="ExternalInput").ap(),
            mask=nc.dram_tensor(f"mask_{w}", [nC, 128, O], U8,
                                kind="ExternalInput").ap(),
            O=O, I=I, nC=nC, n_oc=n_oc,
        )
    b80_in = nc.dram_tensor("b80", [2, 80, 128], BF16, kind="ExternalInput").ap()
    ident_in = nc.dram_tensor("ident", [128, 128], BF16, kind="ExternalInput").ap()
    mtri_in = nc.dram_tensor("mtri", [128, 128], F32, kind="ExternalInput").ap()
    ones1_in = nc.dram_tensor("ones1", [128, 1], BF16, kind="ExternalInput").ap()
    ones1f_in = nc.dram_tensor("ones1f", [1, 128], F32, kind="ExternalInput").ap()

    outT = nc.dram_tensor("outT", [NCH, 128, TOK], F32, kind="ExternalOutput").ap()

    wgS = nc.dram_tensor("wgS", [NIC, 128, NCH, 128], BF16, kind=KIND).ap()
    wuS = nc.dram_tensor("wuS", [NIC, 128, NCH, 128], BF16, kind=KIND).ap()
    wdS = nc.dram_tensor("wdS", [NCH, 128, NIC, 128], BF16, kind=KIND).ap()
    x1S = nc.dram_tensor("x1S", [BPC, NCH, 128, SEQ], F32, kind=KIND).ap()
    intD = nc.dram_tensor("intD", [128, NIC, 1024], BF16, kind=KIND).ap() if dbg else None
    h2S = nc.dram_tensor("h2S", [BPC, NCH, 128, SEQ], BF16, kind=KIND).ap()
    wS = {"g": wgS, "u": wuS, "d": wdS}

    es = ExitStack()
    P = Prog(nc, es)
    for s in ["lsp", "lgp", "waq", "pe", "dv", "ac", "gp", "sy"]:
        P.sem(s)

    def SB(st, name, shape, dt_, side=None):
        return st.enter_context(nc.sbuf_tensor(name, shape, dt_, side=side))

    def PS(st, name, shape, dt_):
        return st.enter_context(nc.psum_tensor(name, shape, dt_))

    ident_sb = SB(es, "ident_sb", [128, 128], BF16)
    mtri_sb = SB(es, "mtri_sb", [128, 128], F32)
    ones1_sb = SB(es, "ones1_sb", [128, 1], BF16)
    ones1f_sb = SB(es, "ones1f_sb", [1, 128], F32)
    b80_sb = SB(es, "b80_sb", [80, 2, 128], BF16)

    with es:
        block = es.enter_context(nc.Block())

        @block.sync
        def _(sync):
            P.inc(sync.dma_start(out=ident_sb[:], in_=ident_in), "lsp", 16)
            P.inc(sync.dma_start(out=mtri_sb[:], in_=mtri_in), "lsp", 16)
            P.inc(sync.dma_start(out=ones1_sb[:], in_=ones1_in), "lsp", 16)
            P.inc(sync.dma_start(out=ones1f_sb[:], in_=ones1f_in), "lsp", 16)
            P.inc(sync.dma_start(out=b80_sb[:],
                                 in_=b80_in.rearrange("b k p -> k b p")), "lsp", 16)

        def full_barrier():
            snap = {k: P.count(k) for k in P.sems}
            for ename in ("sync", "tensor", "vector", "scalar", "gpsimd"):
                def mk(e, snap=snap):
                    for k, n in snap.items():
                        P.wait(e, k, n)
                getattr(block, ename)(mk)

        ph = ExitStack()
        taps_sb = [[SB(ph, f"taps{q}_{s}", [128, 64, 8], F32) for q in range(4)]
                   for s in range(2)]
        upow_sb = [[SB(ph, f"upow{q}_{s}", [128, 64, 4], F32) for q in range(4)]
                   for s in range(2)]
        rt_sb = [[SB(ph, f"rt{q}_{s}", [128, 64, 80], BF16) for q in range(4)]
                 for s in range(2)]
        rtf_sb = SB(ph, "rtf", [128, 64, 20], F32)
        rth_sb = SB(ph, "rth", [128, 64, 20], BF16)
        NMSK = 4
        mask_sb = [SB(ph, f"rmsk{i}", [128, 512], U8) for i in range(NMSK)]
        v1_sb = [SB(ph, f"v1s{i}", [128, 512], BF16) for i in range(2)]
        r80_sb = [SB(ph, f"r80s{i}", [80, 512], BF16) for i in range(2)]
        NOW = 4
        outw_sb = [SB(ph, f"outw{i}", [128, 512], BF16) for i in range(NOW)]
        r80_ps = [PS(ph, f"r80p{i}", [80, 512], BF16) for i in range(2)]
        v0_ps = [PS(ph, f"v0p{i}", [128, 512], F32) for i in range(2)]
        v1_ps = [PS(ph, f"v1p{i}", [128, 512], F32) for i in range(2)]

        obs = []
        for w in ("g", "u", "d"):
            for ob in range(rin[w]["O"] // 512):
                obs.append((w, ob))
        tiles = [(w, ob, C) for (w, ob) in obs for C in range(rin[w]["nC"])]
        NT = len(tiles)
        ob_of_tile = {}
        k = 0
        for oi, (w, ob) in enumerate(obs):
            for C in range(rin[w]["nC"]):
                ob_of_tile[k] = oi
                k += 1

        n_strip_load = {}
        n_rt = {}
        rT_hist = []
        rsb_hist = []
        mm_hist = []
        sel_hist = []
        wr_hist = []
        msk_hist = []
        last_T_of_ob = {}

        def emit_mask_load(t2):
            if t2 >= NT:
                return
            w2, ob2, C2 = tiles[t2]

            @block.sync
            def _(sync, w2=w2, ob2=ob2, C2=C2, t2=t2):
                if t2 >= NMSK:
                    P.wait(sync, "dv", sel_hist[t2 - NMSK])
                P.inc(sync.dma_start(
                    out=mask_sb[t2 % NMSK][:],
                    in_=rin[w2]["mask"][C2, :, ob2 * 512:(ob2 + 1) * 512]),
                    "lsp", 16)
                while len(msk_hist) <= t2:
                    msk_hist.append(0)
                msk_hist[t2] = P.count("lsp")

        def emit_strips_and_rt(oi):
            if oi >= len(obs):
                return
            w, ob = obs[oi]
            sl = oi % 2

            @block.sync
            def _(sync, w=w, ob=ob, oi=oi, sl=sl):
                if oi >= 2 and (oi - 2) in n_rt:
                    P.wait(sync, "gp", n_rt[oi - 2])
                for q in range(4):
                    oc = ob * 4 + q
                    P.inc(sync.dma_start(out=taps_sb[sl][q][:, 0:rin[w]["nC"], :],
                                         in_=rin[w]["taps"][oc]), "lsp", 16)
                    P.inc(sync.dma_start(out=upow_sb[sl][q][:, 0:rin[w]["nC"], :],
                                         in_=rin[w]["upow"][oc]), "lsp", 16)
                n_strip_load[(w, ob)] = P.count("lsp")

            @block.gpsimd
            def _(gpsimd, w=w, ob=ob, oi=oi, sl=sl):
                nC = rin[w]["nC"]
                P.wait(gpsimd, "lsp", n_strip_load[(w, ob)])
                if oi >= 2 and (oi - 2) in last_T_of_ob:
                    P.wait(gpsimd, "pe", last_T_of_ob[oi - 2])
                for q in range(4):
                    i1 = P.inc(gpsimd.tensor_tensor(
                        rtf_sb[:, 0:nC, :].rearrange("p c (j m) -> p c j m", j=4),
                        upow_sb[sl][q][:, 0:nC, :].unsqueeze(3)
                        .broadcast_to([128, nC, 4, 5]),
                        taps_sb[sl][q][:, 0:nC, 0:5].unsqueeze(2)
                        .broadcast_to([128, nC, 4, 5]),
                        ALU.mult), "gp")
                    P.wait(gpsimd, "gp", i1)
                    i2 = P.inc(gpsimd.tensor_copy(rth_sb[:, 0:nC, :],
                                                  rtf_sb[:, 0:nC, :]), "gp")
                    P.wait(gpsimd, "gp", i2)
                    P.inc(gpsimd.tensor_copy(rt_sb[sl][q][:, 0:nC, 0:20],
                                             rth_sb[:, 0:nC, :]), "gp")
                    i4 = P.inc(gpsimd.tensor_tensor(rt_sb[sl][q][:, 0:nC, 20:40],
                                                    rtf_sb[:, 0:nC, :],
                                                    rth_sb[:, 0:nC, :],
                                                    ALU.subtract), "gp")
                    P.wait(gpsimd, "gp", i4)
                    P.inc(gpsimd.tensor_copy(rt_sb[sl][q][:, 0:nC, 40:80],
                                             rt_sb[sl][q][:, 0:nC, 0:40]), "gp")
                n_rt[oi] = P.count("gp")

        def emit_stageA(t):
            w, ob, C = tiles[t]
            oi = ob_of_tile[t]
            par = t % 2
            sl = oi % 2

            @block.tensor
            def _(tensor, w=w, ob=ob, C=C, par=par, t=t, oi=oi, sl=sl):
                P.wait(tensor, "gp", n_rt[oi])
                if t >= 2:
                    P.wait(tensor, "dv", rsb_hist[t - 2])
                for q in range(4):
                    ins = nc.tensor.transpose(
                        r80_ps[par][:, q * 128:(q + 1) * 128],
                        rt_sb[sl][q][:, C, :], ident_sb[:])
                    if q == 3:
                        P.inc(ins, "pe")
                rT_hist.append(P.count("pe"))
                last_T_of_ob[oi] = rT_hist[-1]

            @block.vector
            def _(vector, par=par, t=t):
                P.wait(vector, "pe", rT_hist[t])
                if t >= 2:
                    P.wait(vector, "pe", mm_hist[t - 2])
                P.inc(vector.tensor_copy(r80_sb[par][:], r80_ps[par][:]), "dv")
                rsb_hist.append(P.count("dv"))

        def emit_stageB(t):
            w, ob, C = tiles[t]
            par = t % 2

            @block.tensor
            def _(tensor, par=par, t=t):
                P.wait(tensor, "dv", rsb_hist[t])
                if t >= 2:
                    P.wait(tensor, "dv", sel_hist[t - 2])
                nc.tensor.matmul(v0_ps[par][:], b80_sb[:, 0, :], r80_sb[par][:],
                                 start=True, stop=True)
                P.inc(nc.tensor.matmul(v1_ps[par][:], b80_sb[:, 1, :],
                                       r80_sb[par][:], start=True, stop=True),
                      "pe")
                mm_hist.append(P.count("pe"))

            @block.vector
            def _(vector, par=par, t=t):
                P.wait(vector, "pe", mm_hist[t])
                i1 = P.inc(vector.tensor_copy(v1_sb[par][:], v1_ps[par][:]), "dv")
                P.wait(vector, "lsp", msk_hist[t])
                if t >= NOW:
                    P.wait(vector, "waq", wr_hist[t - NOW])
                P.wait(vector, "dv", i1)
                P.inc(vector.select(outw_sb[t % NOW][:], mask_sb[t % NMSK][:],
                                    v1_sb[par][:], v0_ps[par][:]), "dv")
                sel_hist.append(P.count("dv"))

            @block.scalar
            def _(scalar, w=w, ob=ob, C=C, t=t):
                P.wait(scalar, "dv", sel_hist[t])
                for q in range(4):
                    oc = ob * 4 + q
                    P.inc(scalar.dma_start(
                        out=wS[w][oc, :, C, :],
                        in_=outw_sb[t % NOW][:, q * 128:(q + 1) * 128]),
                        "waq", 16)
                wr_hist.append(P.count("waq"))

        for t2 in range(min(NMSK, NT)):
            emit_mask_load(t2)
        emit_strips_and_rt(0)
        emit_strips_and_rt(1)
        for t in range(NT):
            oi = ob_of_tile[t]
            emit_stageA(t)
            if t + 1 < NT and ob_of_tile[t + 1] == oi + 1:
                emit_strips_and_rt(oi + 2)
            if t >= 1:
                emit_stageB(t - 1)
                emit_mask_load(t - 1 + NMSK)
        emit_stageB(NT - 1)
        emit_mask_load(NT - 1 + NMSK)

        ph.close()
        full_barrier()

        for b in range(BPC):
            tok0 = b * SEQ

            phb = ExitStack()
            h1s = ExitStack()
            h1T_sb = SB(h1s, f"h1T_{b}", [128, NCH, SEQ], BF16)
            zs = ExitStack()
            zall_sb = SB(zs, f"zall_{b}", [128, NH, SEQ], BF16, side="right")

            ln = ExitStack()
            x_sb = SB(ln, f"lx_{b}", [128, NCH, 512], F32)
            xb_sb = [SB(ln, f"lxb{i}_{b}", [128, 512], BF16) for i in range(2)]
            sq_sb = [SB(ln, f"lsq{i}_{b}", [128, 512], BF16) for i in range(2)]
            m_sb = SB(ln, f"lm_{b}", [1, 512], F32)
            var_sb = SB(ln, f"lvar_{b}", [1, 512], F32)
            r_sb = SB(ln, f"lr_{b}", [1, 512], F32)
            mr_sb = SB(ln, f"lmr_{b}", [1, 512], F32)
            rb_sb = SB(ln, f"lrb_{b}", [128, 512], F32)
            mrb_sb = SB(ln, f"lmrb_{b}", [128, 512], F32)
            t1_sb = SB(ln, f"lt1_{b}", [128, 512], F32)
            st_ps = PS(ln, f"lst_{b}", [1, 1024], F32)
            bc_ps = PS(ln, f"lbc_{b}", [128, 1024], F32)

            cast_hist = []
            stat_hist = []

            for tt in range(NSTRIP):
                t0 = tt * 512

                @block.sync
                def _(sync, t0=t0):
                    P.wait(sync, "gp")
                    P.inc(sync.dma_start(
                        out=x_sb[:],
                        in_=xT[:, :, tok0 + t0:tok0 + t0 + 512]
                        .rearrange("c p t -> p c t")), "lsp", 16)
                n_x = P.count("lsp")

                for c in range(NCH):
                    cp = c % 2

                    @block.vector
                    def _(vector, c=c, cp=cp, n_x=n_x):
                        P.wait(vector, "lsp", n_x)
                        if len(stat_hist) >= 2:
                            P.wait(vector, "pe", stat_hist[-2][1])
                        P.inc(vector.tensor_copy(xb_sb[cp][:], x_sb[:, c, :]), "dv")

                    @block.gpsimd
                    def _(gpsimd, c=c, cp=cp, n_x=n_x):
                        P.wait(gpsimd, "lsp", n_x)
                        if len(stat_hist) >= 2:
                            P.wait(gpsimd, "pe", stat_hist[-2][1])
                        P.inc(gpsimd.tensor_tensor(sq_sb[cp][:], x_sb[:, c, :],
                                                   x_sb[:, c, :], ALU.mult), "gp")

                    @block.tensor
                    def _(tensor, c=c, cp=cp):
                        P.wait(tensor, "dv", P.count("dv"))
                        P.wait(tensor, "gp", P.count("gp"))
                        nc.tensor.matmul(st_ps[:, 0:512], ones1_sb[:], xb_sb[cp][:],
                                         start=(c == 0), stop=(c == NCH - 1))
                        ins = nc.tensor.matmul(st_ps[:, 512:1024], ones1_sb[:],
                                               sq_sb[cp][:], start=(c == 0),
                                               stop=(c == NCH - 1))
                        P.inc(ins, "pe")
                        stat_hist.append((c, P.count("pe")))
                n_pe_st = P.count("pe")

                @block.scalar
                def _(scalar, n_pe_st=n_pe_st):
                    P.wait(scalar, "pe", n_pe_st)
                    P.inc(scalar.mul(m_sb[:], st_ps[:, 0:512], 1.0 / D), "ac")
                    P.inc(scalar.mul(var_sb[:], st_ps[:, 512:1024], 1.0 / D), "ac")
                n_ac_m = P.count("ac")

                @block.vector
                def _(vector, n_ac_m=n_ac_m):
                    P.wait(vector, "ac", n_ac_m)
                    i1 = P.inc(vector.tensor_tensor(r_sb[:], m_sb[:], m_sb[:],
                                                    ALU.mult), "dv")
                    P.wait(vector, "dv", i1)
                    i2 = P.inc(vector.tensor_sub(var_sb[:], var_sb[:], r_sb[:]),
                               "dv")
                    P.wait(vector, "dv", i2)
                    P.inc(vector.tensor_scalar_add(var_sb[:], var_sb[:], EPS), "dv")
                n_dv_v = P.count("dv")

                @block.scalar
                def _(scalar, n_dv_v=n_dv_v):
                    P.wait(scalar, "dv", n_dv_v)
                    i1 = P.inc(scalar.activation(out=var_sb[:], in_=var_sb[:],
                                                 func=AF.Ln, bias=0.0, scale=1.0),
                               "ac")
                    P.wait(scalar, "ac", i1)
                    P.inc(scalar.activation(out=r_sb[:], in_=var_sb[:], func=AF.Exp,
                                            bias=0.0, scale=-0.5), "ac")
                n_ac_r = P.count("ac")

                @block.vector
                def _(vector, n_ac_r=n_ac_r):
                    P.wait(vector, "ac", n_ac_r)
                    P.inc(vector.tensor_tensor(mr_sb[:], m_sb[:], r_sb[:], ALU.mult),
                          "dv")
                n_dv_mr = P.count("dv")

                @block.tensor
                def _(tensor, n_dv_mr=n_dv_mr):
                    P.wait(tensor, "dv", n_dv_mr)
                    nc.tensor.matmul(bc_ps[:, 0:512], ones1f_sb[:], r_sb[:],
                                     start=True, stop=True)
                    P.inc(nc.tensor.matmul(bc_ps[:, 512:1024], ones1f_sb[:],
                                           mr_sb[:], start=True, stop=True), "pe")
                n_pe_bc = P.count("pe")

                @block.vector
                def _(vector, n_pe_bc=n_pe_bc):
                    P.wait(vector, "pe", n_pe_bc)
                    P.inc(vector.tensor_copy(rb_sb[:], bc_ps[:, 0:512]), "dv")
                    P.inc(vector.tensor_copy(mrb_sb[:], bc_ps[:, 512:1024]), "dv")
                n_dv_bc = P.count("dv")

                @block.gpsimd
                def _(gpsimd, t0=t0, n_dv_bc=n_dv_bc):
                    P.wait(gpsimd, "dv", n_dv_bc)
                    for c in range(NCH):
                        i1 = P.inc(gpsimd.tensor_tensor(t1_sb[:], x_sb[:, c, :],
                                                        rb_sb[:], ALU.mult), "gp")
                        P.wait(gpsimd, "gp", i1)
                        i2 = P.inc(gpsimd.tensor_tensor(
                            h1T_sb[:, c, t0:t0 + 512], t1_sb[:], mrb_sb[:],
                            ALU.subtract), "gp")
                        P.wait(gpsimd, "gp", i2)

            ln.close()
            full_barrier()

            at = ExitStack()
            wq_sb = [SB(at, f"wq{i}_{b}", [128, NCH, 128], BF16) for i in range(2)]
            wk_sb = [SB(at, f"wk{i}_{b}", [128, NCH, 128], BF16) for i in range(2)]
            wv_sb = [SB(at, f"wv{i}_{b}", [128, NCH, 128], BF16) for i in range(2)]
            qT_sb = SB(at, f"qT_{b}", [128, SEQ], BF16)
            kT_sb = SB(at, f"kT_{b}", [128, SEQ], BF16)
            v_sb = SB(at, f"v_{b}", [128, NIB, 128], BF16)
            scsb = [SB(at, f"scsb{i}_{b}", [128, SEQ], F32) for i in range(2)]
            p_sb = SB(at, f"p_{b}", [128, SEQ], BF16)
            p2_sb = SB(at, f"p2_{b}", [128, SEQ], BF16)
            pt_sb = SB(at, f"pt_{b}", [128, NIB, 512], BF16)
            nm_sb = SB(at, f"nm_{b}", [128, 1], F32)
            pmax_sb = [SB(at, f"pmax{i}_{b}", [128, 1], F32) for i in range(2)]
            sum_sb = SB(at, f"sum_{b}", [128, 1], F32)
            rc_sb = SB(at, f"rc_{b}", [128, 1], F32)

            wld_hist = {}

            def emit_wload(h):
                par = h % 2

                @block.gpsimd
                def _(gpsimd, h=h, par=par):
                    if h >= 2:
                        P.wait(gpsimd, "pe", qkv_hist[h - 2])
                    for win, dst in ((wqT, wq_sb), (wkT, wk_sb), (wvT, wv_sb)):
                        P.inc(gpsimd.dma_start(out=dst[par][:], in_=win[h]),
                              "lgp", 16)
                    wld_hist[h] = P.count("lgp")

            qkv_hist = {}

            emit_wload(0)
            emit_wload(1)

            for h in range(NH):
                par = h % 2

                qk = ExitStack()
                q_ps = [PS(qk, f"q_ps{i}_{b}_{h}", [128, 512], F32)
                        for i in range(2)]
                k_ps = [PS(qk, f"k_ps{i}_{b}_{h}", [128, 512], F32)
                        for i in range(2)]
                v_ps = [PS(qk, f"v_ps{i}_{b}_{h}", [128, 128], F32)
                        for i in range(2)]
                qkev_hist = []
                vev_hist = []

                @block.tensor
                def _(tensor, h=h, par=par):
                    P.wait(tensor, "dv", P.count("dv"))
                    P.wait(tensor, "ac", P.count("ac"))
                    P.wait(tensor, "lgp", wld_hist[h])
                    for tt in range(NSTRIP):
                        qp = tt % 2
                        if tt > 1:
                            P.wait(tensor, "ac", qkev_hist[tt - 2])
                        for c in range(NCH):
                            nc.tensor.matmul(q_ps[qp][:], wq_sb[par][:, c, :],
                                             h1T_sb[:, c, tt * 512:(tt + 1) * 512],
                                             start=(c == 0), stop=(c == NCH - 1))
                        for c in range(NCH):
                            ins = nc.tensor.matmul(
                                k_ps[qp][:], wk_sb[par][:, c, :],
                                h1T_sb[:, c, tt * 512:(tt + 1) * 512],
                                start=(c == 0), stop=(c == NCH - 1))
                        npe = P.inc(ins, "pe")

                        @block.scalar
                        def _(scalar, h=h, tt=tt, qp=qp, npe=npe):
                            P.wait(scalar, "pe", npe)
                            scalar.copy(qT_sb[:, tt * 512:(tt + 1) * 512],
                                        q_ps[qp][:])
                            P.inc(scalar.copy(kT_sb[:, tt * 512:(tt + 1) * 512],
                                              k_ps[qp][:]), "ac")
                            qkev_hist.append(P.count("ac"))

                    for jb in range(NIB):
                        vp = jb % 2
                        if jb >= 2:
                            P.wait(tensor, "dv", vev_hist[jb - 2])
                        for c in range(NCH):
                            ins = nc.tensor.matmul(
                                v_ps[vp][:], h1T_sb[:, c, jb * 128:(jb + 1) * 128],
                                wv_sb[par][:, c, :],
                                start=(c == 0), stop=(c == NCH - 1))
                        npe = P.inc(ins, "pe")

                        @block.vector
                        def _(vector, jb=jb, vp=vp, npe=npe):
                            P.wait(vector, "pe", npe)
                            P.inc(vector.tensor_copy(v_sb[:, jb, :], v_ps[vp][:]),
                                  "dv")
                            vev_hist.append(P.count("dv"))
                    qkv_hist[h] = P.count("pe")

                if h + 2 < NH:
                    emit_wload(h + 2)

                qk.close()

                av = ExitStack()
                scpp = [PS(av, f"scpp{i}_{b}_{h}", [128, 1024], F32)
                        for i in range(2)]
                pt_ps = [PS(av, f"pt_ps{i}_{b}_{h}", [128, 128], BF16)
                         for i in range(2)]
                z_ps = PS(av, f"z_ps_{b}_{h}", [128, 512], F32)

                n_qk_last = qkev_hist[-1]
                cp_hist = []
                pm_hist = {}
                exp_hist = {}
                p2_hist = {}
                ptflat = []

                def emit_scores(h, ib):
                    jext = (ib + 1) * 128
                    par2 = ib % 2
                    chunks = []
                    j0 = 0
                    while j0 < jext:
                        jn = min(1024, jext - j0)
                        chunks.append((j0, jn))
                        j0 += jn
                    chunks = chunks[-1:] + chunks[:-1]
                    pm_hist[ib] = []

                    @block.tensor
                    def _(tensor, h=h, ib=ib, jext=jext, par2=par2):
                        if ib == 0:
                            P.wait(tensor, "ac", n_qk_last)
                            P.wait(tensor, "dv", P.count("dv"))
                        if (ib - 2) in exp_hist:
                            P.wait(tensor, "ac", exp_hist[ib - 2])
                        for ci, (j0, jn) in enumerate(chunks):
                            pp = len(cp_hist) % 2
                            if len(cp_hist) >= 2:
                                P.wait(tensor, "ac", cp_hist[-2])
                            ins = nc.tensor.matmul(
                                scpp[pp][:, 0:min(jn, 512)],
                                qT_sb[:, ib * 128:(ib + 1) * 128],
                                kT_sb[:, j0:j0 + min(jn, 512)],
                                start=True, stop=True)
                            if jn > 512:
                                ins = nc.tensor.matmul(
                                    scpp[pp][:, 512:jn],
                                    qT_sb[:, ib * 128:(ib + 1) * 128],
                                    kT_sb[:, j0 + 512:j0 + jn],
                                    start=True, stop=True)
                            npe = P.inc(ins, "pe")

                            @block.scalar
                            def _(scalar, j0=j0, jn=jn, npe=npe, par2=par2,
                                  pp=pp):
                                P.wait(scalar, "pe", npe)
                                P.inc(scalar.copy(scsb[par2][:, j0:j0 + jn],
                                                  scpp[pp][:, 0:jn]), "ac")
                                cp_hist.append(P.count("ac"))
                            n_cp_c = P.count("ac")

                            @block.vector
                            def _(vector, ib=ib, ci=ci, j0=j0, jn=jn, par2=par2,
                                  jext=jext, n_cp_c=n_cp_c):
                                P.wait(vector, "ac", n_cp_c)
                                if ci == 0:
                                    if (ib - 1) in exp_hist:
                                        P.wait(vector, "ac", exp_hist[ib - 1])
                                    i1 = P.inc(vector.tensor_add(
                                        scsb[par2][:, jext - 128:jext],
                                        scsb[par2][:, jext - 128:jext],
                                        mtri_sb[:]), "dv")
                                    P.wait(vector, "dv", i1)
                                tgt = pmax_sb[0] if ci == 0 else pmax_sb[1]
                                i2 = P.inc(vector.reduce_max(
                                    tgt[:], scsb[par2][:, j0:j0 + jn],
                                    axis=AX.X, negate=False), "dv")
                                if ci > 0:
                                    P.wait(vector, "dv", i2)
                                    P.inc(vector.tensor_max(pmax_sb[0][:],
                                                            pmax_sb[0][:],
                                                            pmax_sb[1][:]), "dv")
                                pm_hist[ib].append(P.count("dv"))

                def emit_softmax(h, ib):
                    jext = (ib + 1) * 128
                    par2 = ib % 2
                    n_pm = pm_hist[ib][-1]

                    @block.vector
                    def _(vector, ib=ib, n_pm=n_pm):
                        P.wait(vector, "dv", n_pm)
                        if (ib - 1) in exp_hist:
                            P.wait(vector, "ac", exp_hist[ib - 1])
                        P.inc(vector.tensor_scalar_mul(nm_sb[:], pmax_sb[0][:],
                                                       -SCALE), "dv")
                    n_dv_nm = P.count("dv")

                    @block.scalar
                    def _(scalar, ib=ib, jext=jext, par2=par2, n_dv_nm=n_dv_nm):
                        P.wait(scalar, "dv", n_dv_nm)
                        if (ib - 1) in p2_hist:
                            P.wait(scalar, "dv", p2_hist[ib - 1])
                        P.inc(scalar.activation(
                            out=p_sb[:, 0:jext], in_=scsb[par2][:, 0:jext],
                            func=AF.Exp, bias=nm_sb[:], scale=SCALE,
                            accum_out=sum_sb[:]), "ac")
                        exp_hist[ib] = P.count("ac")

                    npe_now = P.count("pe")

                    @block.vector
                    def _(vector, ib=ib, jext=jext, npe_now=npe_now):
                        P.wait(vector, "ac", exp_hist[ib])
                        i1 = P.inc(vector.reciprocal(rc_sb[:], sum_sb[:]), "dv")
                        P.wait(vector, "pe", npe_now)
                        P.wait(vector, "dv", i1)
                        P.inc(vector.tensor_scalar_mul(p2_sb[:, 0:jext],
                                                       p_sb[:, 0:jext], rc_sb[:]),
                              "dv")
                        p2_hist[ib] = P.count("dv")

                def emit_transposes(h, ib, ibr):
                    for jb in range(ib + 1):
                        tp = len(ptflat) % 2

                        @block.tensor
                        def _(tensor, ib=ib, jb=jb, tp=tp):
                            P.wait(tensor, "dv", p2_hist[ib])
                            if len(ptflat) >= 2:
                                P.wait(tensor, "dv", ptflat[-2])
                            P.inc(nc.tensor.transpose(
                                pt_ps[tp][:], p2_sb[:, jb * 128:(jb + 1) * 128],
                                ident_sb[:]), "pe")
                        npe = P.count("pe")

                        @block.vector
                        def _(vector, ib=ib, jb=jb, tp=tp, ibr=ibr, npe=npe):
                            P.wait(vector, "pe", npe)
                            P.inc(vector.tensor_copy(
                                pt_sb[:, jb, ibr * 128:(ibr + 1) * 128],
                                pt_ps[tp][:]), "dv")
                            ptflat.append(P.count("dv"))

                emit_scores(h, 0)
                for istrip in range(NSTRIP):
                    ib0 = istrip * 4
                    for ibr in range(4):
                        ib = ib0 + ibr
                        emit_softmax(h, ib)
                        if ib + 1 < NIB:
                            emit_scores(h, ib + 1)
                        emit_transposes(h, ib, ibr)

                    n_pt_last = P.count("dv")

                    @block.tensor
                    def _(tensor, ib0=ib0, n_pt_last=n_pt_last):
                        P.wait(tensor, "dv", n_pt_last)
                        njb = ib0 + 4
                        for jb in range(njb):
                            rel0 = max(0, jb - ib0) * 128
                            ins = nc.tensor.matmul(
                                z_ps[:, rel0:512], v_sb[:, jb, :],
                                pt_sb[:, jb, rel0:512],
                                start=(jb == 0), stop=(jb == njb - 1))
                        P.inc(ins, "pe")
                    n_pe_z = P.count("pe")

                    @block.vector
                    def _(vector, h=h, istrip=istrip, n_pe_z=n_pe_z):
                        P.wait(vector, "pe", n_pe_z)
                        P.inc(vector.tensor_copy(
                            zall_sb[:, h, istrip * 512:(istrip + 1) * 512],
                            z_ps[:]), "dv")

                av.close()

            at.close()
            h1s.close()
            full_barrier()

            op = ExitStack()
            wo_sb = SB(op, f"wo_{b}", [128, NCH, NCH, 128], BF16)
            wtmp2_sb = SB(op, f"wtmp2_{b}", [128, 2048], F32)
            x_sb = SB(op, f"ox_{b}", [128, NCH, 512], F32)
            xb_sb = [SB(op, f"oxb{i}_{b}", [128, 512], BF16) for i in range(2)]
            sq_sb = [SB(op, f"osq{i}_{b}", [128, 512], BF16) for i in range(2)]
            m_sb = SB(op, f"om_{b}", [1, 512], F32)
            var_sb = SB(op, f"ovar_{b}", [1, 512], F32)
            r_sb = SB(op, f"or_{b}", [1, 512], F32)
            mr_sb = SB(op, f"omr_{b}", [1, 512], F32)
            rb_sb = SB(op, f"orb_{b}", [128, 512], F32)
            mrb_sb = SB(op, f"omrb_{b}", [128, 512], F32)
            t1_sb = SB(op, f"ot1_{b}", [128, 512], F32)
            h2full_sb = SB(op, f"oh2f_{b}", [128, NCH, 512], BF16)
            ao_ps = [PS(op, f"ao_ps{i}_{b}", [128, 512], F32) for i in range(2)]
            st_ps = PS(op, f"ost_{b}", [1, 1024], F32)
            bc_ps = PS(op, f"obc_{b}", [128, 1024], F32)

            for wc in range(NCH):
                @block.sync
                def _(sync, wc=wc):
                    P.wait(sync, "dv", P.count("dv"))
                    P.inc(sync.dma_start(out=wtmp2_sb[:],
                                         in_=woT[wc].rearrange("p a b -> p (a b)")),
                          "lsp", 16)
                nwl = P.count("lsp")

                @block.vector
                def _(vector, wc=wc, nwl=nwl):
                    P.wait(vector, "lsp", nwl)
                    P.inc(vector.tensor_copy(
                        wo_sb[:, wc].rearrange("p a b -> p (a b)"), wtmp2_sb[:]),
                        "dv")
            n_wo = P.count("dv")

            x1_hist = []
            n_x1wr = [0]
            for tt in range(NSTRIP):
                t0 = tt * 512
                sq_hist = []

                for dc in range(NCH):
                    par = dc % 2
                    cp = dc % 2

                    @block.sync
                    def _(sync, dc=dc, t0=t0):
                        P.wait(sync, "gp")
                        P.wait(sync, "waq", n_x1wr[0])
                        P.inc(sync.dma_start(
                            out=x_sb[:, dc, :],
                            in_=xT[dc, :, tok0 + t0:tok0 + t0 + 512]), "lsp", 16)
                    n_x = P.count("lsp")

                    @block.tensor
                    def _(tensor, dc=dc, par=par, t0=t0, n_wo=n_wo):
                        P.wait(tensor, "dv", n_wo)
                        if len(x1_hist) >= 2:
                            P.wait(tensor, "dv", x1_hist[-2])
                        for hh in range(NH):
                            ins = nc.tensor.matmul(
                                ao_ps[par][:], wo_sb[:, dc, hh, :],
                                zall_sb[:, hh, t0:t0 + 512],
                                start=(hh == 0), stop=(hh == NH - 1))
                        P.inc(ins, "pe")
                    n_pe_ao = P.count("pe")

                    @block.vector
                    def _(vector, dc=dc, par=par, n_pe_ao=n_pe_ao, n_x=n_x):
                        P.wait(vector, "pe", n_pe_ao)
                        P.wait(vector, "lsp", n_x)
                        P.inc(vector.tensor_add(x_sb[:, dc, :], ao_ps[par][:],
                                                x_sb[:, dc, :]), "dv")
                        x1_hist.append(P.count("dv"))
                    n_x1 = P.count("dv")

                    @block.vector
                    def _(vector, dc=dc, cp=cp, n_x1=n_x1):
                        if len(sq_hist) >= 2:
                            P.wait(vector, "pe", sq_hist[-2][1])
                        P.inc(vector.tensor_copy(xb_sb[cp][:], x_sb[:, dc, :]),
                              "dv")

                    @block.gpsimd
                    def _(gpsimd, dc=dc, cp=cp, n_x1=n_x1):
                        P.wait(gpsimd, "dv", n_x1)
                        if len(sq_hist) >= 2:
                            P.wait(gpsimd, "pe", sq_hist[-2][1])
                        P.inc(gpsimd.tensor_tensor(sq_sb[cp][:], x_sb[:, dc, :],
                                                   x_sb[:, dc, :], ALU.mult), "gp")
                    n_cast = (P.count("dv"), P.count("gp"))

                    @block.tensor
                    def _(tensor, dc=dc, cp=cp, n_cast=n_cast):
                        P.wait(tensor, "dv", n_cast[0])
                        P.wait(tensor, "gp", n_cast[1])
                        nc.tensor.matmul(st_ps[:, 0:512], ones1_sb[:], xb_sb[cp][:],
                                         start=(dc == 0), stop=(dc == NCH - 1))
                        ins = nc.tensor.matmul(st_ps[:, 512:1024], ones1_sb[:],
                                               sq_sb[cp][:], start=(dc == 0),
                                               stop=(dc == NCH - 1))
                        P.inc(ins, "pe")
                        sq_hist.append((dc, P.count("pe")))
                n_pe_st = P.count("pe")

                @block.scalar
                def _(scalar, n_pe_st=n_pe_st):
                    P.wait(scalar, "pe", n_pe_st)
                    P.inc(scalar.mul(m_sb[:], st_ps[:, 0:512], 1.0 / D), "ac")
                    P.inc(scalar.mul(var_sb[:], st_ps[:, 512:1024], 1.0 / D), "ac")
                n_ac_m = P.count("ac")

                @block.vector
                def _(vector, n_ac_m=n_ac_m):
                    P.wait(vector, "ac", n_ac_m)
                    i1 = P.inc(vector.tensor_tensor(r_sb[:], m_sb[:], m_sb[:],
                                                    ALU.mult), "dv")
                    P.wait(vector, "dv", i1)
                    i2 = P.inc(vector.tensor_sub(var_sb[:], var_sb[:], r_sb[:]),
                               "dv")
                    P.wait(vector, "dv", i2)
                    P.inc(vector.tensor_scalar_add(var_sb[:], var_sb[:], EPS), "dv")
                n_dv_v = P.count("dv")

                @block.scalar
                def _(scalar, n_dv_v=n_dv_v):
                    P.wait(scalar, "dv", n_dv_v)
                    i1 = P.inc(scalar.activation(out=var_sb[:], in_=var_sb[:],
                                                 func=AF.Ln, bias=0.0, scale=1.0),
                               "ac")
                    P.wait(scalar, "ac", i1)
                    P.inc(scalar.activation(out=r_sb[:], in_=var_sb[:], func=AF.Exp,
                                            bias=0.0, scale=-0.5), "ac")
                n_ac_r = P.count("ac")

                @block.vector
                def _(vector, n_ac_r=n_ac_r):
                    P.wait(vector, "ac", n_ac_r)
                    P.inc(vector.tensor_tensor(mr_sb[:], m_sb[:], r_sb[:], ALU.mult),
                          "dv")
                n_dv_mr = P.count("dv")

                @block.tensor
                def _(tensor, n_dv_mr=n_dv_mr):
                    P.wait(tensor, "dv", n_dv_mr)
                    nc.tensor.matmul(bc_ps[:, 0:512], ones1f_sb[:], r_sb[:],
                                     start=True, stop=True)
                    P.inc(nc.tensor.matmul(bc_ps[:, 512:1024], ones1f_sb[:],
                                           mr_sb[:], start=True, stop=True), "pe")
                n_pe_bc = P.count("pe")

                @block.vector
                def _(vector, n_pe_bc=n_pe_bc):
                    P.wait(vector, "pe", n_pe_bc)
                    P.inc(vector.tensor_copy(rb_sb[:], bc_ps[:, 0:512]), "dv")
                    P.inc(vector.tensor_copy(mrb_sb[:], bc_ps[:, 512:1024]), "dv")
                n_dv_bc = P.count("dv")

                n_x1_all = P.count("dv")

                @block.scalar
                def _(scalar, t0=t0, n_x1_all=n_x1_all):
                    P.wait(scalar, "dv", n_x1_all)
                    P.inc(scalar.dma_start(
                        out=x1S[b, :, :, t0:t0 + 512].rearrange("c p t -> p c t"),
                        in_=x_sb[:]), "waq", 16)
                n_x1wr[0] = P.count("waq")

                @block.gpsimd
                def _(gpsimd, t0=t0, n_dv_bc=n_dv_bc):
                    P.wait(gpsimd, "dv", n_dv_bc)
                    for c in range(NCH):
                        i1 = P.inc(gpsimd.tensor_tensor(t1_sb[:], x_sb[:, c, :],
                                                        rb_sb[:], ALU.mult), "gp")
                        P.wait(gpsimd, "gp", i1)
                        i2 = P.inc(gpsimd.tensor_tensor(h2full_sb[:, c, :],
                                                        t1_sb[:],
                                                        mrb_sb[:], ALU.subtract),
                                   "gp")
                        P.wait(gpsimd, "gp", i2)
                ngp_all = P.count("gp")

                @block.scalar
                def _(scalar, t0=t0, ngp_all=ngp_all):
                    P.wait(scalar, "gp", ngp_all)
                    P.inc(scalar.dma_start(
                        out=h2S[b, :, :, t0:t0 + 512].rearrange("c p t -> p c t"),
                        in_=h2full_sb[:]), "waq", 16)
                n_x1wr[0] = P.count("waq")

            op.close()
            zs.close()
            full_barrier()

            ff = ExitStack()
            inter_sb = SB(ff, f"fint_{b}", [128, NIC, 1024], BF16)
            wg_sb = [SB(ff, f"fwg{i}_{b}", [128, NCH, 128], BF16) for i in range(2)]
            wu_sb = [SB(ff, f"fwu{i}_{b}", [128, NCH, 128], BF16) for i in range(2)]
            sg_sb = SB(ff, f"fsg_{b}", [128, 1024], BF16)

            for ftt in range(BPC):
                t0 = ftt * 1024

                gu = ExitStack()
                h2_sb = SB(gu, f"fh2_{b}_{ftt}", [128, NCH, 1024], BF16)
                gate_ps = [PS(gu, f"g_ps{i}_{b}_{ftt}", [128, 1024], F32)
                           for i in range(2)]
                up_ps = [PS(gu, f"u_ps{i}_{b}_{ftt}", [128, 1024], F32)
                         for i in range(2)]

                @block.sync
                def _(sync, t0=t0):
                    P.wait(sync, "waq")
                    P.wait(sync, "pe")
                    P.inc(sync.dma_start(
                        out=h2_sb[:],
                        in_=h2S[b, :, :, t0:t0 + 1024]
                        .rearrange("c p t -> p c t")), "lsp", 16)
                n_h2 = P.count("lsp")

                n_wg = {}
                sgm_hist = []

                @block.sync
                def _(sync):
                    for ic in range(2):
                        P.inc(sync.dma_start(out=wg_sb[ic % 2][:], in_=wgS[ic]),
                              "lsp", 16)
                        P.inc(sync.dma_start(out=wu_sb[ic % 2][:], in_=wuS[ic]),
                              "lsp", 16)
                        n_wg[ic] = P.count("lsp")

                for ic in range(NIC):
                    par = ic % 2

                    @block.tensor
                    def _(tensor, ic=ic, par=par, n_h2=n_h2):
                        P.wait(tensor, "lsp", n_wg[ic])
                        P.wait(tensor, "lsp", n_h2)
                        if len(sgm_hist) >= 2:
                            P.wait(tensor, "dv", sgm_hist[-2])
                        elif ic < 2:
                            P.wait(tensor, "dv", P.count("dv"))
                        for th in range(2):
                            for c in range(NCH):
                                nc.tensor.matmul(
                                    gate_ps[par][:, th * 512:(th + 1) * 512],
                                    wg_sb[par][:, c, :],
                                    h2_sb[:, c, th * 512:(th + 1) * 512],
                                    start=(c == 0), stop=(c == NCH - 1))
                        for th in range(2):
                            for c in range(NCH):
                                ins = nc.tensor.matmul(
                                    up_ps[par][:, th * 512:(th + 1) * 512],
                                    wu_sb[par][:, c, :],
                                    h2_sb[:, c, th * 512:(th + 1) * 512],
                                    start=(c == 0), stop=(c == NCH - 1))
                        P.inc(ins, "pe")
                    n_pe_g = P.count("pe")

                    @block.sync
                    def _(sync, ic=ic, n_pe_g=n_pe_g):
                        if ic + 2 < NIC:
                            P.wait(sync, "pe", n_pe_g)
                            P.inc(sync.dma_start(out=wg_sb[ic % 2][:],
                                                 in_=wgS[ic + 2]), "lsp", 16)
                            P.inc(sync.dma_start(out=wu_sb[ic % 2][:],
                                                 in_=wuS[ic + 2]), "lsp", 16)
                            n_wg[ic + 2] = P.count("lsp")

                    @block.scalar
                    def _(scalar, par=par, n_pe_g=n_pe_g):
                        P.wait(scalar, "pe", n_pe_g)
                        if sgm_hist:
                            P.wait(scalar, "dv", sgm_hist[-1])
                        P.inc(scalar.activation(out=sg_sb[:], in_=gate_ps[par][:],
                                                func=AF.Silu, bias=0.0, scale=1.0),
                              "ac")
                    n_ac_sg = P.count("ac")

                    @block.vector
                    def _(vector, ic=ic, par=par, n_ac_sg=n_ac_sg):
                        P.wait(vector, "ac", n_ac_sg)
                        P.inc(vector.tensor_tensor(inter_sb[:, ic, :], sg_sb[:],
                                                   up_ps[par][:], ALU.mult), "dv")
                        sgm_hist.append(P.count("dv"))

                gu.close()

                dn = ExitStack()
                wd_sb = [SB(dn, f"fwd{i}_{b}_{ftt}", [128, NIC, 128], BF16)
                         for i in range(2)]
                x1_sb = SB(dn, f"fx1_{b}_{ftt}", [128, 1024], F32)
                yo_sb = SB(dn, f"fyo_{b}_{ftt}", [128, 1024], F32)
                y_ps = [PS(dn, f"y_ps{i}_{b}_{ftt}", [128, 1024], F32)
                        for i in range(2)]
                n_inter_done = P.count("dv")
                if intD is not None and b == 0 and ftt == 0:
                    @block.scalar
                    def _(scalar):
                        P.wait(scalar, "dv", n_inter_done)
                        P.inc(scalar.dma_start(out=intD, in_=inter_sb[:]), "waq", 16)
                n_wd = {}
                yev_hist = []

                @block.sync
                def _(sync):
                    P.wait(sync, "pe")
                    P.wait(sync, "dv")
                    for dc in range(2):
                        P.inc(sync.dma_start(out=wd_sb[dc % 2][:], in_=wdS[dc]),
                              "lsp", 16)
                        n_wd[dc] = P.count("lsp")

                for dc in range(NCH):
                    par = dc % 2

                    @block.tensor
                    def _(tensor, dc=dc, par=par):
                        P.wait(tensor, "lsp", n_wd[dc])
                        P.wait(tensor, "dv", n_inter_done)
                        if len(yev_hist) >= 2:
                            P.wait(tensor, "dv", yev_hist[-2])
                        for th in range(2):
                            for cc in range(NIC):
                                ins = nc.tensor.matmul(
                                    y_ps[par][:, th * 512:(th + 1) * 512],
                                    wd_sb[par][:, cc, :],
                                    inter_sb[:, cc, th * 512:(th + 1) * 512],
                                    start=(cc == 0), stop=(cc == NIC - 1))
                        P.inc(ins, "pe")
                    n_pe_y = P.count("pe")

                    @block.sync
                    def _(sync, dc=dc, t0=t0, n_pe_y=n_pe_y):
                        if dc + 2 < NCH:
                            P.wait(sync, "pe", n_pe_y)
                            P.inc(sync.dma_start(out=wd_sb[dc % 2][:],
                                                 in_=wdS[dc + 2]), "lsp", 16)
                            n_wd[dc + 2] = P.count("lsp")
                        if yev_hist:
                            P.wait(sync, "dv", yev_hist[-1])
                        P.inc(sync.dma_start(out=x1_sb[:],
                                             in_=x1S[b, dc, :, t0:t0 + 1024]),
                              "lsp", 16)
                    n_x1l = P.count("lsp")

                    @block.vector
                    def _(vector, par=par, n_pe_y=n_pe_y, n_x1l=n_x1l):
                        P.wait(vector, "pe", n_pe_y)
                        P.wait(vector, "lsp", n_x1l)
                        P.wait(vector, "waq")
                        P.inc(vector.tensor_add(yo_sb[:], y_ps[par][:], x1_sb[:]),
                              "dv")
                        yev_hist.append(P.count("dv"))
                    n_ye = P.count("dv")

                    @block.scalar
                    def _(scalar, dc=dc, t0=t0, n_ye=n_ye):
                        P.wait(scalar, "dv", n_ye)
                        P.inc(scalar.dma_start(
                            out=outT[dc, :, tok0 + t0:tok0 + t0 + 1024],
                            in_=yo_sb[:]), "waq", 16)

                dn.close()

            ff.close()
            phb.close()
            full_barrier()

        @block.sync
        def _(sync):
            P.wait(sync, "waq")
            P.wait(sync, "sy")

    return nc



def _host_prepare(inputs):
    x = np.asarray(inputs["L0_attn_x"], dtype=np.float32)
    W_Q = np.asarray(inputs["L0_attn_W_Q"], dtype=np.float32)
    W_K = np.asarray(inputs["L0_attn_W_K"], dtype=np.float32)
    W_V = np.asarray(inputs["L0_attn_W_V"], dtype=np.float32)
    W_O = np.asarray(inputs["L0_attn_W_O"], dtype=np.float32)

    def qkvT(W):
        return np.ascontiguousarray(
            W.transpose(0, 2, 1).reshape(NH, NCH, 128, DH).transpose(0, 2, 1, 3))

    wqT = qkvT(W_Q)
    wkT = qkvT(W_K)
    wvT = qkvT(W_V)
    woT = np.ascontiguousarray(
        W_O.reshape(NH, NCH, 128, DH).transpose(1, 3, 0, 2))

    consts = dict(
        b80=_build_B80(),
        ident=np.eye(128, dtype=np.float32).astype(bfdt),
        mtri=np.triu(np.full((128, 128), -1e9, dtype=np.float32), k=1),
        ones1=np.ones((128, 1), dtype=np.float32).astype(bfdt),
        ones1f=np.ones((1, 128), dtype=np.float32),
    )
    rec = {}
    for w, (O, I), cpname in (("g", (INTER, D), "L0_ffn_gate_cp"),
                              ("u", (INTER, D), "L0_ffn_up_cp"),
                              ("d", (D, INTER), "L0_ffn_down_cp")):
        cp = np.asarray(inputs[cpname], dtype=np.float32)
        tapidx, upow, mask = _get_recon(O, I)
        taps = np.zeros(tapidx.shape[:-1] + (8,), dtype=np.float32)
        taps[..., 0:5] = cp[tapidx]
        rec[f"taps_{w}"] = taps
        rec[f"upow_{w}"] = upow
        rec[f"mask_{w}"] = mask

    per_core = []
    for c in range(NCORES):
        xs = x[BPC * c:BPC * (c + 1)]
        xTc = np.ascontiguousarray(
            xs.transpose(2, 0, 1).reshape(D, TOK)).reshape(NCH, 128, TOK)
        im = dict(xT=xTc, wqT=wqT, wkT=wkT, wvT=wvT, woT=woT, **consts, **rec)
        per_core.append(im)
    return per_core


def kernel(**inputs):
    per_core = _host_prepare(inputs)
    nc = build_program()
    res = run_bass_kernel_spmd(nc, per_core, core_ids=list(range(NCORES)))
    out = np.empty((NBATCH, SEQ, D), dtype=np.float32)
    for c in range(NCORES):
        oT = res.results[c]["outT"].reshape(D, BPC, SEQ)
        out[BPC * c:BPC * (c + 1)] = oT.transpose(1, 2, 0)
    return out


# version 15
# speedup vs baseline: 1.0134x; 1.0134x over previous
import math
from contextlib import ExitStack

import numpy as np
import ml_dtypes

import concourse.bass as bass
import concourse.mybir as mybir
from concourse.bass_utils import run_bass_kernel_spmd

F32 = mybir.dt.float32
BF16 = mybir.dt.bfloat16
U8 = mybir.dt.uint8
AF = mybir.ActivationFunctionType
ALU = mybir.AluOpType
AX = mybir.AxisListType
bfdt = ml_dtypes.bfloat16

D = 2048
SEQ = 2048
NBATCH = 16
NH = 16
DH = 128
INTER = 8192
N_CTRL = 130156
NCORES = 8
BPC = NBATCH // NCORES
TOK = BPC * SEQ
NCH = D // 128
NIC = INTER // 128
EPS = 1e-5
SCALE = 1.0 / math.sqrt(DH)
NIB = SEQ // 128
NSTRIP = SEQ // 512

NSAMP = INTER * D
DT64 = (N_CTRL - 1.0) / (NSAMP - 1.0)

CK = np.array([
    [0.0, -0.5, 1.0, -0.5],
    [1.0, 0.0, -2.5, 1.5],
    [0.0, 0.5, 2.0, -1.5],
    [0.0, 0.0, -0.5, 0.5],
], dtype=np.float64)


def _build_B80():
    p = np.arange(128, dtype=np.float64)
    out = []
    for beta in (0, 1):
        B = np.zeros((128, 20), dtype=np.float64)
        a = p * DT64 - beta
        for k in range(4):
            al = CK[k]
            g = [al[0] + al[1] * a + al[2] * a * a + al[3] * a * a * a,
                 al[1] + 2 * al[2] * a + 3 * al[3] * a * a,
                 al[2] + 3 * al[3] * a,
                 np.full(128, al[3])]
            m = k + beta
            for j in range(4):
                B[:, j * 5 + m] += g[j]
        Bh = B.astype(bfdt).astype(np.float64)
        Bl = (B - Bh).astype(np.float32).astype(bfdt).astype(np.float64)
        B80 = np.concatenate([Bh, Bh, Bl, Bl], axis=1)
        out.append(np.ascontiguousarray(B80.T).astype(np.float32).astype(bfdt))
    return np.stack(out)


def _recon_consts(out_size, in_size):
    nC = in_size // 128
    n = out_size * in_size
    t = np.linspace(0.0, N_CTRL - 1.0, n, dtype=np.float64)
    i_arr = np.clip(np.floor(t).astype(np.int64), 0, N_CTRL - 2)
    o = np.arange(out_size, dtype=np.int64)
    C = np.arange(nC, dtype=np.int64)
    n0 = o[None, :] * in_size + 128 * C[:, None]
    S = t[n0]
    I0 = i_arr[n0]
    u = S - I0
    upow = np.stack([np.ones_like(u), u, u * u, u * u * u], axis=-1).astype(np.float32)
    idx128 = n0[:, :, None] + np.arange(128)[None, None, :]
    s = (i_arr[idx128] - I0[:, :, None]).astype(np.uint8)
    tapidx = np.clip(I0[:, :, None] + np.arange(-1, 4)[None, None, :], 0, N_CTRL - 1)
    n_oc = out_size // 128
    tapidx = tapidx.transpose(1, 0, 2).reshape(n_oc, 128, nC, 5)
    upow = upow.transpose(1, 0, 2).reshape(n_oc, 128, nC, 4)
    mask = np.ascontiguousarray(s.transpose(0, 2, 1))
    return tapidx, upow, mask


_RECON_CACHE = {}


def _get_recon(out_size, in_size):
    key = (out_size, in_size)
    if key not in _RECON_CACHE:
        _RECON_CACHE[key] = _recon_consts(out_size, in_size)
    return _RECON_CACHE[key]


class Prog:
    def __init__(self, nc, es):
        self.nc = nc
        self.es = es
        self.sems = {}

    def sem(self, name):
        s = self.es.enter_context(self.nc.semaphore(name))
        self.sems[name] = [s, 0]

    def inc(self, inst, name, d=1):
        s = self.sems[name]
        inst.then_inc(s[0], d)
        s[1] += d
        return s[1]

    def count(self, name):
        return self.sems[name][1]

    def wait(self, eng, name, n=None):
        s = self.sems[name]
        n = s[1] if n is None else n
        if n > 0:
            eng.wait_ge(s[0], n)


def build_program(dbg=False):
    nc = bass.Bass("TRN2", target_bir_lowering=False)
    KIND = "ExternalOutput" if dbg else "Internal"

    xT = nc.dram_tensor("xT", [NCH, 128, TOK], F32, kind="ExternalInput").ap()
    wqT = nc.dram_tensor("wqT", [NH, 128, NCH, DH], F32, kind="ExternalInput").ap()
    wkT = nc.dram_tensor("wkT", [NH, 128, NCH, DH], F32, kind="ExternalInput").ap()
    wvT = nc.dram_tensor("wvT", [NH, 128, NCH, DH], F32, kind="ExternalInput").ap()
    woT = nc.dram_tensor("woT", [NCH, 128, NCH, 128], F32, kind="ExternalInput").ap()
    rin = {}
    for w, (O, I) in (("g", (INTER, D)), ("u", (INTER, D)), ("d", (D, INTER))):
        nC = I // 128
        n_oc = O // 128
        rin[w] = dict(
            taps=nc.dram_tensor(f"taps_{w}", [n_oc, 128, nC, 8], F32,
                                kind="ExternalInput").ap(),
            upow=nc.dram_tensor(f"upow_{w}", [n_oc, 128, nC, 4], F32,
                                kind="ExternalInput").ap(),
            mask=nc.dram_tensor(f"mask_{w}", [nC, 128, O], U8,
                                kind="ExternalInput").ap(),
            O=O, I=I, nC=nC, n_oc=n_oc,
        )
    b80_in = nc.dram_tensor("b80", [2, 80, 128], BF16, kind="ExternalInput").ap()
    ident_in = nc.dram_tensor("ident", [128, 128], BF16, kind="ExternalInput").ap()
    mtri_in = nc.dram_tensor("mtri", [128, 128], F32, kind="ExternalInput").ap()
    ones1_in = nc.dram_tensor("ones1", [128, 1], BF16, kind="ExternalInput").ap()
    ones1f_in = nc.dram_tensor("ones1f", [1, 128], F32, kind="ExternalInput").ap()

    outT = nc.dram_tensor("outT", [NCH, 128, TOK], F32, kind="ExternalOutput").ap()

    wgS = nc.dram_tensor("wgS", [NIC, 128, NCH, 128], BF16, kind=KIND).ap()
    wuS = nc.dram_tensor("wuS", [NIC, 128, NCH, 128], BF16, kind=KIND).ap()
    wdS = nc.dram_tensor("wdS", [NCH, 128, NIC, 128], BF16, kind=KIND).ap()
    x1S = nc.dram_tensor("x1S", [BPC, NCH, 128, SEQ], F32, kind=KIND).ap()
    intD = nc.dram_tensor("intD", [128, NIC, 1024], BF16, kind=KIND).ap() if dbg else None
    h2S = nc.dram_tensor("h2S", [BPC, NCH, 128, SEQ], BF16, kind=KIND).ap()
    wS = {"g": wgS, "u": wuS, "d": wdS}

    es = ExitStack()
    P = Prog(nc, es)
    for s in ["lsp", "lgp", "waq", "pe", "dv", "ac", "gp", "sy"]:
        P.sem(s)

    def SB(st, name, shape, dt_, side=None):
        return st.enter_context(nc.sbuf_tensor(name, shape, dt_, side=side))

    def PS(st, name, shape, dt_):
        return st.enter_context(nc.psum_tensor(name, shape, dt_))

    ident_sb = SB(es, "ident_sb", [128, 128], BF16)
    mtri_sb = SB(es, "mtri_sb", [128, 128], F32)
    ones1_sb = SB(es, "ones1_sb", [128, 1], BF16)
    ones1f_sb = SB(es, "ones1f_sb", [1, 128], F32)
    b80_sb = SB(es, "b80_sb", [80, 2, 128], BF16)

    with es:
        block = es.enter_context(nc.Block())

        @block.sync
        def _(sync):
            P.inc(sync.dma_start(out=ident_sb[:], in_=ident_in), "lsp", 16)
            P.inc(sync.dma_start(out=mtri_sb[:], in_=mtri_in), "lsp", 16)
            P.inc(sync.dma_start(out=ones1_sb[:], in_=ones1_in), "lsp", 16)
            P.inc(sync.dma_start(out=ones1f_sb[:], in_=ones1f_in), "lsp", 16)
            P.inc(sync.dma_start(out=b80_sb[:],
                                 in_=b80_in.rearrange("b k p -> k b p")), "lsp", 16)

        def full_barrier():
            snap = {k: P.count(k) for k in P.sems}
            for ename in ("sync", "tensor", "vector", "scalar", "gpsimd"):
                def mk(e, snap=snap):
                    for k, n in snap.items():
                        P.wait(e, k, n)
                getattr(block, ename)(mk)

        ph = ExitStack()
        taps_sb = [[SB(ph, f"taps{q}_{s}", [128, 64, 8], F32) for q in range(4)]
                   for s in range(2)]
        upow_sb = [[SB(ph, f"upow{q}_{s}", [128, 64, 4], F32) for q in range(4)]
                   for s in range(2)]
        rt_sb = [[SB(ph, f"rt{q}_{s}", [128, 64, 80], BF16) for q in range(4)]
                 for s in range(2)]
        rtf_sb = SB(ph, "rtf", [128, 64, 20], F32)
        rth_sb = SB(ph, "rth", [128, 64, 20], BF16)
        NMSK = 4
        mask_sb = [SB(ph, f"rmsk{i}", [128, 512], U8) for i in range(NMSK)]
        v1_sb = [SB(ph, f"v1s{i}", [128, 512], BF16) for i in range(2)]
        r80_sb = [SB(ph, f"r80s{i}", [80, 512], BF16) for i in range(2)]
        NOW = 4
        outw_sb = [SB(ph, f"outw{i}", [128, 512], BF16) for i in range(NOW)]
        r80_ps = [PS(ph, f"r80p{i}", [80, 512], BF16) for i in range(2)]
        v0_ps = [PS(ph, f"v0p{i}", [128, 512], F32) for i in range(2)]
        v1_ps = [PS(ph, f"v1p{i}", [128, 512], F32) for i in range(2)]

        obs = []
        for w in ("g", "u", "d"):
            for ob in range(rin[w]["O"] // 512):
                obs.append((w, ob))
        tiles = [(w, ob, C) for (w, ob) in obs for C in range(rin[w]["nC"])]
        NT = len(tiles)
        ob_of_tile = {}
        k = 0
        for oi, (w, ob) in enumerate(obs):
            for C in range(rin[w]["nC"]):
                ob_of_tile[k] = oi
                k += 1

        n_strip_load = {}
        n_rt = {}
        rT_hist = []
        rsb_hist = []
        mm_hist = []
        sel_hist = []
        wr_hist = []
        msk_hist = []
        last_T_of_ob = {}

        def emit_mask_load(t2):
            if t2 >= NT:
                return
            w2, ob2, C2 = tiles[t2]

            @block.sync
            def _(sync, w2=w2, ob2=ob2, C2=C2, t2=t2):
                if t2 >= NMSK:
                    P.wait(sync, "dv", sel_hist[t2 - NMSK])
                P.inc(sync.dma_start(
                    out=mask_sb[t2 % NMSK][:],
                    in_=rin[w2]["mask"][C2, :, ob2 * 512:(ob2 + 1) * 512]),
                    "lsp", 16)
                while len(msk_hist) <= t2:
                    msk_hist.append(0)
                msk_hist[t2] = P.count("lsp")

        def emit_strips_and_rt(oi):
            if oi >= len(obs):
                return
            w, ob = obs[oi]
            sl = oi % 2

            @block.sync
            def _(sync, w=w, ob=ob, oi=oi, sl=sl):
                if oi >= 2 and (oi - 2) in n_rt:
                    P.wait(sync, "gp", n_rt[oi - 2])
                for q in range(4):
                    oc = ob * 4 + q
                    P.inc(sync.dma_start(out=taps_sb[sl][q][:, 0:rin[w]["nC"], :],
                                         in_=rin[w]["taps"][oc]), "lsp", 16)
                    P.inc(sync.dma_start(out=upow_sb[sl][q][:, 0:rin[w]["nC"], :],
                                         in_=rin[w]["upow"][oc]), "lsp", 16)
                n_strip_load[(w, ob)] = P.count("lsp")

            @block.gpsimd
            def _(gpsimd, w=w, ob=ob, oi=oi, sl=sl):
                nC = rin[w]["nC"]
                P.wait(gpsimd, "lsp", n_strip_load[(w, ob)])
                if oi >= 2 and (oi - 2) in last_T_of_ob:
                    P.wait(gpsimd, "pe", last_T_of_ob[oi - 2])
                for q in range(4):
                    i1 = P.inc(gpsimd.tensor_tensor(
                        rtf_sb[:, 0:nC, :].rearrange("p c (j m) -> p c j m", j=4),
                        upow_sb[sl][q][:, 0:nC, :].unsqueeze(3)
                        .broadcast_to([128, nC, 4, 5]),
                        taps_sb[sl][q][:, 0:nC, 0:5].unsqueeze(2)
                        .broadcast_to([128, nC, 4, 5]),
                        ALU.mult), "gp")
                    P.wait(gpsimd, "gp", i1)
                    i2 = P.inc(gpsimd.tensor_copy(rth_sb[:, 0:nC, :],
                                                  rtf_sb[:, 0:nC, :]), "gp")
                    P.wait(gpsimd, "gp", i2)
                    P.inc(gpsimd.tensor_copy(rt_sb[sl][q][:, 0:nC, 0:20],
                                             rth_sb[:, 0:nC, :]), "gp")
                    i4 = P.inc(gpsimd.tensor_tensor(rt_sb[sl][q][:, 0:nC, 20:40],
                                                    rtf_sb[:, 0:nC, :],
                                                    rth_sb[:, 0:nC, :],
                                                    ALU.subtract), "gp")
                    P.wait(gpsimd, "gp", i4)
                    P.inc(gpsimd.tensor_copy(rt_sb[sl][q][:, 0:nC, 40:80],
                                             rt_sb[sl][q][:, 0:nC, 0:40]), "gp")
                n_rt[oi] = P.count("gp")

        def emit_stageA(t):
            w, ob, C = tiles[t]
            oi = ob_of_tile[t]
            par = t % 2
            sl = oi % 2

            @block.tensor
            def _(tensor, w=w, ob=ob, C=C, par=par, t=t, oi=oi, sl=sl):
                P.wait(tensor, "gp", n_rt[oi])
                if t >= 2:
                    P.wait(tensor, "dv", rsb_hist[t - 2])
                for q in range(4):
                    ins = nc.tensor.transpose(
                        r80_ps[par][:, q * 128:(q + 1) * 128],
                        rt_sb[sl][q][:, C, :], ident_sb[:])
                    if q == 3:
                        P.inc(ins, "pe")
                rT_hist.append(P.count("pe"))
                last_T_of_ob[oi] = rT_hist[-1]

            @block.vector
            def _(vector, par=par, t=t):
                P.wait(vector, "pe", rT_hist[t])
                if t >= 2:
                    P.wait(vector, "pe", mm_hist[t - 2])
                P.inc(vector.tensor_copy(r80_sb[par][:], r80_ps[par][:]), "dv")
                rsb_hist.append(P.count("dv"))

        def emit_stageB(t):
            w, ob, C = tiles[t]
            par = t % 2

            @block.tensor
            def _(tensor, par=par, t=t):
                P.wait(tensor, "dv", rsb_hist[t])
                if t >= 2:
                    P.wait(tensor, "dv", sel_hist[t - 2])
                nc.tensor.matmul(v0_ps[par][:], b80_sb[:, 0, :], r80_sb[par][:],
                                 start=True, stop=True)
                P.inc(nc.tensor.matmul(v1_ps[par][:], b80_sb[:, 1, :],
                                       r80_sb[par][:], start=True, stop=True),
                      "pe")
                mm_hist.append(P.count("pe"))

            @block.vector
            def _(vector, par=par, t=t):
                P.wait(vector, "pe", mm_hist[t])
                i1 = P.inc(vector.tensor_copy(v1_sb[par][:], v1_ps[par][:]), "dv")
                P.wait(vector, "lsp", msk_hist[t])
                if t >= NOW:
                    P.wait(vector, "waq", wr_hist[t - NOW])
                P.wait(vector, "dv", i1)
                P.inc(vector.select(outw_sb[t % NOW][:], mask_sb[t % NMSK][:],
                                    v1_sb[par][:], v0_ps[par][:]), "dv")
                sel_hist.append(P.count("dv"))

            @block.scalar
            def _(scalar, w=w, ob=ob, C=C, t=t):
                P.wait(scalar, "dv", sel_hist[t])
                for q in range(4):
                    oc = ob * 4 + q
                    P.inc(scalar.dma_start(
                        out=wS[w][oc, :, C, :],
                        in_=outw_sb[t % NOW][:, q * 128:(q + 1) * 128]),
                        "waq", 16)
                wr_hist.append(P.count("waq"))

        for t2 in range(min(NMSK, NT)):
            emit_mask_load(t2)
        emit_strips_and_rt(0)
        emit_strips_and_rt(1)
        for t in range(NT):
            oi = ob_of_tile[t]
            emit_stageA(t)
            if t + 1 < NT and ob_of_tile[t + 1] == oi + 1:
                emit_strips_and_rt(oi + 2)
            if t >= 1:
                emit_stageB(t - 1)
                emit_mask_load(t - 1 + NMSK)
        emit_stageB(NT - 1)
        emit_mask_load(NT - 1 + NMSK)

        ph.close()
        full_barrier()

        for b in range(BPC):
            tok0 = b * SEQ

            phb = ExitStack()
            h1s = ExitStack()
            h1T_sb = SB(h1s, f"h1T_{b}", [128, NCH, SEQ], BF16)
            zs = ExitStack()
            zall_sb = SB(zs, f"zall_{b}", [128, NH, SEQ], BF16, side="right")

            ln = ExitStack()
            x_sb = SB(ln, f"lx_{b}", [128, NCH, 512], F32)
            xb_sb = [SB(ln, f"lxb{i}_{b}", [128, NCH, 512], BF16)
                     for i in range(1)]
            sq_sb = [SB(ln, f"lsq{i}_{b}", [128, NCH, 512], BF16)
                     for i in range(1)]
            m_sb = SB(ln, f"lm_{b}", [1, 512], F32)
            var_sb = SB(ln, f"lvar_{b}", [1, 512], F32)
            r_sb = SB(ln, f"lr_{b}", [1, 512], F32)
            mr_sb = SB(ln, f"lmr_{b}", [1, 512], F32)
            rb_sb = SB(ln, f"lrb_{b}", [128, 512], F32)
            mrb_sb = SB(ln, f"lmrb_{b}", [128, 512], F32)
            t1_sb = SB(ln, f"lt1_{b}", [128, 512], F32)
            st_ps = PS(ln, f"lst_{b}", [1, 1024], F32)
            bc_ps = PS(ln, f"lbc_{b}", [128, 1024], F32)

            cast_hist = []
            stat_hist = []

            for tt in range(NSTRIP):
                t0 = tt * 512

                @block.sync
                def _(sync, t0=t0):
                    P.wait(sync, "gp")
                    P.wait(sync, "dv")
                    P.inc(sync.dma_start(
                        out=x_sb[:],
                        in_=xT[:, :, tok0 + t0:tok0 + t0 + 512]
                        .rearrange("c p t -> p c t")), "lsp", 16)
                n_x = P.count("lsp")

                @block.vector
                def _(vector, n_x=n_x):
                    P.wait(vector, "lsp", n_x)
                    if stat_hist:
                        P.wait(vector, "pe", stat_hist[-1][1])
                    P.inc(vector.tensor_copy(xb_sb[0][:], x_sb[:]), "dv")
                n_xb = P.count("dv")

                @block.gpsimd
                def _(gpsimd, n_x=n_x):
                    P.wait(gpsimd, "lsp", n_x)
                    if stat_hist:
                        P.wait(gpsimd, "pe", stat_hist[-1][1])
                    P.inc(gpsimd.tensor_tensor(sq_sb[0][:], x_sb[:], x_sb[:],
                                               ALU.mult), "gp")
                n_sq = P.count("gp")

                @block.tensor
                def _(tensor, n_xb=n_xb, n_sq=n_sq):
                    P.wait(tensor, "dv", n_xb)
                    P.wait(tensor, "gp", n_sq)
                    for c in range(NCH):
                        nc.tensor.matmul(st_ps[:, 0:512], ones1_sb[:],
                                         xb_sb[0][:, c, :],
                                         start=(c == 0), stop=(c == NCH - 1))
                    for c in range(NCH):
                        ins = nc.tensor.matmul(st_ps[:, 512:1024], ones1_sb[:],
                                               sq_sb[0][:, c, :],
                                               start=(c == 0),
                                               stop=(c == NCH - 1))
                    P.inc(ins, "pe")
                    stat_hist.append((tt, P.count("pe")))
                n_pe_st = P.count("pe")

                @block.scalar
                def _(scalar, n_pe_st=n_pe_st):
                    P.wait(scalar, "pe", n_pe_st)
                    P.inc(scalar.mul(m_sb[:], st_ps[:, 0:512], 1.0 / D), "ac")
                    P.inc(scalar.mul(var_sb[:], st_ps[:, 512:1024], 1.0 / D), "ac")
                n_ac_m = P.count("ac")

                @block.vector
                def _(vector, n_ac_m=n_ac_m):
                    P.wait(vector, "ac", n_ac_m)
                    i1 = P.inc(vector.tensor_tensor(r_sb[:], m_sb[:], m_sb[:],
                                                    ALU.mult), "dv")
                    P.wait(vector, "dv", i1)
                    i2 = P.inc(vector.tensor_sub(var_sb[:], var_sb[:], r_sb[:]),
                               "dv")
                    P.wait(vector, "dv", i2)
                    P.inc(vector.tensor_scalar_add(var_sb[:], var_sb[:], EPS), "dv")
                n_dv_v = P.count("dv")

                @block.scalar
                def _(scalar, n_dv_v=n_dv_v):
                    P.wait(scalar, "dv", n_dv_v)
                    i1 = P.inc(scalar.activation(out=var_sb[:], in_=var_sb[:],
                                                 func=AF.Ln, bias=0.0, scale=1.0),
                               "ac")
                    P.wait(scalar, "ac", i1)
                    P.inc(scalar.activation(out=r_sb[:], in_=var_sb[:], func=AF.Exp,
                                            bias=0.0, scale=-0.5), "ac")
                n_ac_r = P.count("ac")

                @block.vector
                def _(vector, n_ac_r=n_ac_r):
                    P.wait(vector, "ac", n_ac_r)
                    P.inc(vector.tensor_tensor(mr_sb[:], m_sb[:], r_sb[:], ALU.mult),
                          "dv")
                n_dv_mr = P.count("dv")

                @block.tensor
                def _(tensor, n_dv_mr=n_dv_mr):
                    P.wait(tensor, "dv", n_dv_mr)
                    nc.tensor.matmul(bc_ps[:, 0:512], ones1f_sb[:], r_sb[:],
                                     start=True, stop=True)
                    P.inc(nc.tensor.matmul(bc_ps[:, 512:1024], ones1f_sb[:],
                                           mr_sb[:], start=True, stop=True), "pe")
                n_pe_bc = P.count("pe")

                @block.vector
                def _(vector, n_pe_bc=n_pe_bc):
                    P.wait(vector, "pe", n_pe_bc)
                    P.inc(vector.tensor_copy(rb_sb[:], bc_ps[:, 0:512]), "dv")
                    P.inc(vector.tensor_copy(mrb_sb[:], bc_ps[:, 512:1024]), "dv")
                n_dv_bc = P.count("dv")

                @block.gpsimd
                def _(gpsimd, t0=t0, n_dv_bc=n_dv_bc):
                    P.wait(gpsimd, "dv", n_dv_bc)
                    for c in range(NCH):
                        i1 = P.inc(gpsimd.tensor_tensor(t1_sb[:], x_sb[:, c, :],
                                                        rb_sb[:], ALU.mult), "gp")
                        P.wait(gpsimd, "gp", i1)
                        i2 = P.inc(gpsimd.tensor_tensor(
                            h1T_sb[:, c, t0:t0 + 512], t1_sb[:], mrb_sb[:],
                            ALU.subtract), "gp")
                        P.wait(gpsimd, "gp", i2)

            ln.close()
            full_barrier()

            at = ExitStack()
            wq_sb = [SB(at, f"wq{i}_{b}", [128, NCH, 128], BF16) for i in range(2)]
            wk_sb = [SB(at, f"wk{i}_{b}", [128, NCH, 128], BF16) for i in range(2)]
            wv_sb = [SB(at, f"wv{i}_{b}", [128, NCH, 128], BF16) for i in range(2)]
            qT_sb = SB(at, f"qT_{b}", [128, SEQ], BF16)
            kT_sb = SB(at, f"kT_{b}", [128, SEQ], BF16)
            v_sb = SB(at, f"v_{b}", [128, NIB, 128], BF16)
            scsb = [SB(at, f"scsb{i}_{b}", [128, SEQ], F32) for i in range(2)]
            p_sb = SB(at, f"p_{b}", [128, SEQ], BF16)
            p2_sb = SB(at, f"p2_{b}", [128, SEQ], BF16)
            pt_sb = SB(at, f"pt_{b}", [128, NIB, 512], BF16)
            nm_sb = SB(at, f"nm_{b}", [128, 1], F32)
            pmax_sb = [SB(at, f"pmax{i}_{b}", [128, 1], F32) for i in range(2)]
            sum_sb = SB(at, f"sum_{b}", [128, 1], F32)
            rc_sb = SB(at, f"rc_{b}", [128, 1], F32)

            wld_hist = {}

            def emit_wload(h):
                par = h % 2

                @block.gpsimd
                def _(gpsimd, h=h, par=par):
                    if h >= 2:
                        P.wait(gpsimd, "pe", qkv_hist[h - 2])
                    for win, dst in ((wqT, wq_sb), (wkT, wk_sb), (wvT, wv_sb)):
                        P.inc(gpsimd.dma_start(out=dst[par][:], in_=win[h]),
                              "lgp", 16)
                    wld_hist[h] = P.count("lgp")

            qkv_hist = {}

            emit_wload(0)
            emit_wload(1)

            for h in range(NH):
                par = h % 2

                qk = ExitStack()
                q_ps = [PS(qk, f"q_ps{i}_{b}_{h}", [128, 512], F32)
                        for i in range(2)]
                k_ps = [PS(qk, f"k_ps{i}_{b}_{h}", [128, 512], F32)
                        for i in range(2)]
                v_ps = [PS(qk, f"v_ps{i}_{b}_{h}", [128, 128], F32)
                        for i in range(2)]
                qkev_hist = []
                vev_hist = []

                @block.tensor
                def _(tensor, h=h, par=par):
                    P.wait(tensor, "dv", P.count("dv"))
                    P.wait(tensor, "ac", P.count("ac"))
                    P.wait(tensor, "lgp", wld_hist[h])
                    for tt in range(NSTRIP):
                        qp = tt % 2
                        if tt > 1:
                            P.wait(tensor, "ac", qkev_hist[tt - 2])
                        for c in range(NCH):
                            nc.tensor.matmul(q_ps[qp][:], wq_sb[par][:, c, :],
                                             h1T_sb[:, c, tt * 512:(tt + 1) * 512],
                                             start=(c == 0), stop=(c == NCH - 1))
                        for c in range(NCH):
                            ins = nc.tensor.matmul(
                                k_ps[qp][:], wk_sb[par][:, c, :],
                                h1T_sb[:, c, tt * 512:(tt + 1) * 512],
                                start=(c == 0), stop=(c == NCH - 1))
                        npe = P.inc(ins, "pe")

                        @block.scalar
                        def _(scalar, h=h, tt=tt, qp=qp, npe=npe):
                            P.wait(scalar, "pe", npe)
                            scalar.copy(qT_sb[:, tt * 512:(tt + 1) * 512],
                                        q_ps[qp][:])
                            P.inc(scalar.copy(kT_sb[:, tt * 512:(tt + 1) * 512],
                                              k_ps[qp][:]), "ac")
                            qkev_hist.append(P.count("ac"))

                    for jb in range(NIB):
                        vp = jb % 2
                        if jb >= 2:
                            P.wait(tensor, "dv", vev_hist[jb - 2])
                        for c in range(NCH):
                            ins = nc.tensor.matmul(
                                v_ps[vp][:], h1T_sb[:, c, jb * 128:(jb + 1) * 128],
                                wv_sb[par][:, c, :],
                                start=(c == 0), stop=(c == NCH - 1))
                        npe = P.inc(ins, "pe")

                        @block.vector
                        def _(vector, jb=jb, vp=vp, npe=npe):
                            P.wait(vector, "pe", npe)
                            P.inc(vector.tensor_copy(v_sb[:, jb, :], v_ps[vp][:]),
                                  "dv")
                            vev_hist.append(P.count("dv"))
                    qkv_hist[h] = P.count("pe")

                if h + 2 < NH:
                    emit_wload(h + 2)

                qk.close()

                av = ExitStack()
                scpp = [PS(av, f"scpp{i}_{b}_{h}", [128, 1024], F32)
                        for i in range(2)]
                pt_ps = [PS(av, f"pt_ps{i}_{b}_{h}", [128, 128], BF16)
                         for i in range(2)]
                z_ps = PS(av, f"z_ps_{b}_{h}", [128, 512], F32)

                n_qk_last = qkev_hist[-1]
                cp_hist = []
                pm_hist = {}
                exp_hist = {}
                p2_hist = {}
                ptflat = []

                def emit_scores(h, ib):
                    jext = (ib + 1) * 128
                    par2 = ib % 2
                    chunks = []
                    j0 = 0
                    while j0 < jext:
                        jn = min(1024, jext - j0)
                        chunks.append((j0, jn))
                        j0 += jn
                    chunks = chunks[-1:] + chunks[:-1]
                    pm_hist[ib] = []

                    @block.tensor
                    def _(tensor, h=h, ib=ib, jext=jext, par2=par2):
                        if ib == 0:
                            P.wait(tensor, "ac", n_qk_last)
                            P.wait(tensor, "dv", P.count("dv"))
                        if (ib - 2) in exp_hist:
                            P.wait(tensor, "ac", exp_hist[ib - 2])
                        for ci, (j0, jn) in enumerate(chunks):
                            pp = len(cp_hist) % 2
                            if len(cp_hist) >= 2:
                                P.wait(tensor, "ac", cp_hist[-2])
                            ins = nc.tensor.matmul(
                                scpp[pp][:, 0:min(jn, 512)],
                                qT_sb[:, ib * 128:(ib + 1) * 128],
                                kT_sb[:, j0:j0 + min(jn, 512)],
                                start=True, stop=True)
                            if jn > 512:
                                ins = nc.tensor.matmul(
                                    scpp[pp][:, 512:jn],
                                    qT_sb[:, ib * 128:(ib + 1) * 128],
                                    kT_sb[:, j0 + 512:j0 + jn],
                                    start=True, stop=True)
                            npe = P.inc(ins, "pe")

                            @block.scalar
                            def _(scalar, j0=j0, jn=jn, npe=npe, par2=par2,
                                  pp=pp):
                                P.wait(scalar, "pe", npe)
                                P.inc(scalar.copy(scsb[par2][:, j0:j0 + jn],
                                                  scpp[pp][:, 0:jn]), "ac")
                                cp_hist.append(P.count("ac"))
                            n_cp_c = P.count("ac")

                            @block.vector
                            def _(vector, ib=ib, ci=ci, j0=j0, jn=jn, par2=par2,
                                  jext=jext, n_cp_c=n_cp_c):
                                P.wait(vector, "ac", n_cp_c)
                                if ci == 0:
                                    if (ib - 1) in exp_hist:
                                        P.wait(vector, "ac", exp_hist[ib - 1])
                                    i1 = P.inc(vector.tensor_add(
                                        scsb[par2][:, jext - 128:jext],
                                        scsb[par2][:, jext - 128:jext],
                                        mtri_sb[:]), "dv")
                                    P.wait(vector, "dv", i1)
                                tgt = pmax_sb[0] if ci == 0 else pmax_sb[1]
                                i2 = P.inc(vector.reduce_max(
                                    tgt[:], scsb[par2][:, j0:j0 + jn],
                                    axis=AX.X, negate=False), "dv")
                                if ci > 0:
                                    P.wait(vector, "dv", i2)
                                    P.inc(vector.tensor_max(pmax_sb[0][:],
                                                            pmax_sb[0][:],
                                                            pmax_sb[1][:]), "dv")
                                pm_hist[ib].append(P.count("dv"))

                def emit_softmax(h, ib):
                    jext = (ib + 1) * 128
                    par2 = ib % 2
                    n_pm = pm_hist[ib][-1]

                    @block.vector
                    def _(vector, ib=ib, n_pm=n_pm):
                        P.wait(vector, "dv", n_pm)
                        if (ib - 1) in exp_hist:
                            P.wait(vector, "ac", exp_hist[ib - 1])
                        P.inc(vector.tensor_scalar_mul(nm_sb[:], pmax_sb[0][:],
                                                       -SCALE), "dv")
                    n_dv_nm = P.count("dv")

                    @block.scalar
                    def _(scalar, ib=ib, jext=jext, par2=par2, n_dv_nm=n_dv_nm):
                        P.wait(scalar, "dv", n_dv_nm)
                        if (ib - 1) in p2_hist:
                            P.wait(scalar, "dv", p2_hist[ib - 1])
                        P.inc(scalar.activation(
                            out=p_sb[:, 0:jext], in_=scsb[par2][:, 0:jext],
                            func=AF.Exp, bias=nm_sb[:], scale=SCALE,
                            accum_out=sum_sb[:]), "ac")
                        exp_hist[ib] = P.count("ac")

                    npe_now = P.count("pe")

                    @block.vector
                    def _(vector, ib=ib, jext=jext, npe_now=npe_now):
                        P.wait(vector, "ac", exp_hist[ib])
                        i1 = P.inc(vector.reciprocal(rc_sb[:], sum_sb[:]), "dv")
                        P.wait(vector, "pe", npe_now)
                        P.wait(vector, "dv", i1)
                        P.inc(vector.tensor_scalar_mul(p2_sb[:, 0:jext],
                                                       p_sb[:, 0:jext], rc_sb[:]),
                              "dv")
                        p2_hist[ib] = P.count("dv")

                def emit_transposes(h, ib, ibr):
                    for jb in range(ib + 1):
                        tp = len(ptflat) % 2

                        @block.tensor
                        def _(tensor, ib=ib, jb=jb, tp=tp):
                            P.wait(tensor, "dv", p2_hist[ib])
                            if len(ptflat) >= 2:
                                P.wait(tensor, "dv", ptflat[-2])
                            P.inc(nc.tensor.transpose(
                                pt_ps[tp][:], p2_sb[:, jb * 128:(jb + 1) * 128],
                                ident_sb[:]), "pe")
                        npe = P.count("pe")

                        @block.vector
                        def _(vector, ib=ib, jb=jb, tp=tp, ibr=ibr, npe=npe):
                            P.wait(vector, "pe", npe)
                            P.inc(vector.tensor_copy(
                                pt_sb[:, jb, ibr * 128:(ibr + 1) * 128],
                                pt_ps[tp][:]), "dv")
                            ptflat.append(P.count("dv"))

                emit_scores(h, 0)
                for istrip in range(NSTRIP):
                    ib0 = istrip * 4
                    for ibr in range(4):
                        ib = ib0 + ibr
                        emit_softmax(h, ib)
                        if ib + 1 < NIB:
                            emit_scores(h, ib + 1)
                        emit_transposes(h, ib, ibr)

                    n_pt_last = P.count("dv")

                    @block.tensor
                    def _(tensor, ib0=ib0, n_pt_last=n_pt_last):
                        P.wait(tensor, "dv", n_pt_last)
                        njb = ib0 + 4
                        for jb in range(njb):
                            rel0 = max(0, jb - ib0) * 128
                            ins = nc.tensor.matmul(
                                z_ps[:, rel0:512], v_sb[:, jb, :],
                                pt_sb[:, jb, rel0:512],
                                start=(jb == 0), stop=(jb == njb - 1))
                        P.inc(ins, "pe")
                    n_pe_z = P.count("pe")

                    @block.vector
                    def _(vector, h=h, istrip=istrip, n_pe_z=n_pe_z):
                        P.wait(vector, "pe", n_pe_z)
                        P.inc(vector.tensor_copy(
                            zall_sb[:, h, istrip * 512:(istrip + 1) * 512],
                            z_ps[:]), "dv")

                av.close()

            at.close()
            h1s.close()
            full_barrier()

            op = ExitStack()
            wo_sb = SB(op, f"wo_{b}", [128, NCH, NCH, 128], BF16)
            wtmp2_sb = SB(op, f"wtmp2_{b}", [128, 2048], F32)
            x_sb = SB(op, f"ox_{b}", [128, NCH, 512], F32)
            xb_sb = [SB(op, f"oxb{i}_{b}", [128, 512], BF16) for i in range(2)]
            sq_sb = [SB(op, f"osq{i}_{b}", [128, 512], BF16) for i in range(2)]
            m_sb = SB(op, f"om_{b}", [1, 512], F32)
            var_sb = SB(op, f"ovar_{b}", [1, 512], F32)
            r_sb = SB(op, f"or_{b}", [1, 512], F32)
            mr_sb = SB(op, f"omr_{b}", [1, 512], F32)
            rb_sb = SB(op, f"orb_{b}", [128, 512], F32)
            mrb_sb = SB(op, f"omrb_{b}", [128, 512], F32)
            t1_sb = SB(op, f"ot1_{b}", [128, 512], F32)
            h2full_sb = SB(op, f"oh2f_{b}", [128, NCH, 512], BF16)
            ao_ps = [PS(op, f"ao_ps{i}_{b}", [128, 512], F32) for i in range(2)]
            st_ps = PS(op, f"ost_{b}", [1, 1024], F32)
            bc_ps = PS(op, f"obc_{b}", [128, 1024], F32)

            for wc in range(NCH):
                @block.sync
                def _(sync, wc=wc):
                    P.wait(sync, "dv", P.count("dv"))
                    P.inc(sync.dma_start(out=wtmp2_sb[:],
                                         in_=woT[wc].rearrange("p a b -> p (a b)")),
                          "lsp", 16)
                nwl = P.count("lsp")

                @block.vector
                def _(vector, wc=wc, nwl=nwl):
                    P.wait(vector, "lsp", nwl)
                    P.inc(vector.tensor_copy(
                        wo_sb[:, wc].rearrange("p a b -> p (a b)"), wtmp2_sb[:]),
                        "dv")
            n_wo = P.count("dv")

            x1_hist = []
            n_x1wr = [0]
            for tt in range(NSTRIP):
                t0 = tt * 512
                sq_hist = []

                for dc in range(NCH):
                    par = dc % 2
                    cp = dc % 2

                    @block.sync
                    def _(sync, dc=dc, t0=t0):
                        P.wait(sync, "gp")
                        P.wait(sync, "waq", n_x1wr[0])
                        P.inc(sync.dma_start(
                            out=x_sb[:, dc, :],
                            in_=xT[dc, :, tok0 + t0:tok0 + t0 + 512]), "lsp", 16)
                    n_x = P.count("lsp")

                    @block.tensor
                    def _(tensor, dc=dc, par=par, t0=t0, n_wo=n_wo):
                        P.wait(tensor, "dv", n_wo)
                        if len(x1_hist) >= 2:
                            P.wait(tensor, "dv", x1_hist[-2])
                        for hh in range(NH):
                            ins = nc.tensor.matmul(
                                ao_ps[par][:], wo_sb[:, dc, hh, :],
                                zall_sb[:, hh, t0:t0 + 512],
                                start=(hh == 0), stop=(hh == NH - 1))
                        P.inc(ins, "pe")
                    n_pe_ao = P.count("pe")

                    @block.vector
                    def _(vector, dc=dc, par=par, n_pe_ao=n_pe_ao, n_x=n_x):
                        P.wait(vector, "pe", n_pe_ao)
                        P.wait(vector, "lsp", n_x)
                        P.inc(vector.tensor_add(x_sb[:, dc, :], ao_ps[par][:],
                                                x_sb[:, dc, :]), "dv")
                        x1_hist.append(P.count("dv"))
                    n_x1 = P.count("dv")

                    @block.vector
                    def _(vector, dc=dc, cp=cp, n_x1=n_x1):
                        if len(sq_hist) >= 2:
                            P.wait(vector, "pe", sq_hist[-2][1])
                        P.inc(vector.tensor_copy(xb_sb[cp][:], x_sb[:, dc, :]),
                              "dv")

                    @block.gpsimd
                    def _(gpsimd, dc=dc, cp=cp, n_x1=n_x1):
                        P.wait(gpsimd, "dv", n_x1)
                        if len(sq_hist) >= 2:
                            P.wait(gpsimd, "pe", sq_hist[-2][1])
                        P.inc(gpsimd.tensor_tensor(sq_sb[cp][:], x_sb[:, dc, :],
                                                   x_sb[:, dc, :], ALU.mult), "gp")
                    n_cast = (P.count("dv"), P.count("gp"))

                    @block.tensor
                    def _(tensor, dc=dc, cp=cp, n_cast=n_cast):
                        P.wait(tensor, "dv", n_cast[0])
                        P.wait(tensor, "gp", n_cast[1])
                        nc.tensor.matmul(st_ps[:, 0:512], ones1_sb[:], xb_sb[cp][:],
                                         start=(dc == 0), stop=(dc == NCH - 1))
                        ins = nc.tensor.matmul(st_ps[:, 512:1024], ones1_sb[:],
                                               sq_sb[cp][:], start=(dc == 0),
                                               stop=(dc == NCH - 1))
                        P.inc(ins, "pe")
                        sq_hist.append((dc, P.count("pe")))
                n_pe_st = P.count("pe")

                @block.scalar
                def _(scalar, n_pe_st=n_pe_st):
                    P.wait(scalar, "pe", n_pe_st)
                    P.inc(scalar.mul(m_sb[:], st_ps[:, 0:512], 1.0 / D), "ac")
                    P.inc(scalar.mul(var_sb[:], st_ps[:, 512:1024], 1.0 / D), "ac")
                n_ac_m = P.count("ac")

                @block.vector
                def _(vector, n_ac_m=n_ac_m):
                    P.wait(vector, "ac", n_ac_m)
                    i1 = P.inc(vector.tensor_tensor(r_sb[:], m_sb[:], m_sb[:],
                                                    ALU.mult), "dv")
                    P.wait(vector, "dv", i1)
                    i2 = P.inc(vector.tensor_sub(var_sb[:], var_sb[:], r_sb[:]),
                               "dv")
                    P.wait(vector, "dv", i2)
                    P.inc(vector.tensor_scalar_add(var_sb[:], var_sb[:], EPS), "dv")
                n_dv_v = P.count("dv")

                @block.scalar
                def _(scalar, n_dv_v=n_dv_v):
                    P.wait(scalar, "dv", n_dv_v)
                    i1 = P.inc(scalar.activation(out=var_sb[:], in_=var_sb[:],
                                                 func=AF.Ln, bias=0.0, scale=1.0),
                               "ac")
                    P.wait(scalar, "ac", i1)
                    P.inc(scalar.activation(out=r_sb[:], in_=var_sb[:], func=AF.Exp,
                                            bias=0.0, scale=-0.5), "ac")
                n_ac_r = P.count("ac")

                @block.vector
                def _(vector, n_ac_r=n_ac_r):
                    P.wait(vector, "ac", n_ac_r)
                    P.inc(vector.tensor_tensor(mr_sb[:], m_sb[:], r_sb[:], ALU.mult),
                          "dv")
                n_dv_mr = P.count("dv")

                @block.tensor
                def _(tensor, n_dv_mr=n_dv_mr):
                    P.wait(tensor, "dv", n_dv_mr)
                    nc.tensor.matmul(bc_ps[:, 0:512], ones1f_sb[:], r_sb[:],
                                     start=True, stop=True)
                    P.inc(nc.tensor.matmul(bc_ps[:, 512:1024], ones1f_sb[:],
                                           mr_sb[:], start=True, stop=True), "pe")
                n_pe_bc = P.count("pe")

                @block.vector
                def _(vector, n_pe_bc=n_pe_bc):
                    P.wait(vector, "pe", n_pe_bc)
                    P.inc(vector.tensor_copy(rb_sb[:], bc_ps[:, 0:512]), "dv")
                    P.inc(vector.tensor_copy(mrb_sb[:], bc_ps[:, 512:1024]), "dv")
                n_dv_bc = P.count("dv")

                n_x1_all = P.count("dv")

                @block.scalar
                def _(scalar, t0=t0, n_x1_all=n_x1_all):
                    P.wait(scalar, "dv", n_x1_all)
                    P.inc(scalar.dma_start(
                        out=x1S[b, :, :, t0:t0 + 512].rearrange("c p t -> p c t"),
                        in_=x_sb[:]), "waq", 16)
                n_x1wr[0] = P.count("waq")

                @block.gpsimd
                def _(gpsimd, t0=t0, n_dv_bc=n_dv_bc):
                    P.wait(gpsimd, "dv", n_dv_bc)
                    for c in range(NCH):
                        i1 = P.inc(gpsimd.tensor_tensor(t1_sb[:], x_sb[:, c, :],
                                                        rb_sb[:], ALU.mult), "gp")
                        P.wait(gpsimd, "gp", i1)
                        i2 = P.inc(gpsimd.tensor_tensor(h2full_sb[:, c, :],
                                                        t1_sb[:],
                                                        mrb_sb[:], ALU.subtract),
                                   "gp")
                        P.wait(gpsimd, "gp", i2)
                ngp_all = P.count("gp")

                @block.scalar
                def _(scalar, t0=t0, ngp_all=ngp_all):
                    P.wait(scalar, "gp", ngp_all)
                    P.inc(scalar.dma_start(
                        out=h2S[b, :, :, t0:t0 + 512].rearrange("c p t -> p c t"),
                        in_=h2full_sb[:]), "waq", 16)
                n_x1wr[0] = P.count("waq")

            op.close()
            zs.close()
            full_barrier()

            ff = ExitStack()
            inter_sb = SB(ff, f"fint_{b}", [128, NIC, 1024], BF16)
            wg_sb = [SB(ff, f"fwg{i}_{b}", [128, NCH, 128], BF16) for i in range(2)]
            wu_sb = [SB(ff, f"fwu{i}_{b}", [128, NCH, 128], BF16) for i in range(2)]
            sg_sb = SB(ff, f"fsg_{b}", [128, 1024], BF16)

            for ftt in range(BPC):
                t0 = ftt * 1024

                gu = ExitStack()
                h2_sb = SB(gu, f"fh2_{b}_{ftt}", [128, NCH, 1024], BF16)
                gate_ps = [PS(gu, f"g_ps{i}_{b}_{ftt}", [128, 1024], F32)
                           for i in range(2)]
                up_ps = [PS(gu, f"u_ps{i}_{b}_{ftt}", [128, 1024], F32)
                         for i in range(2)]

                @block.sync
                def _(sync, t0=t0):
                    P.wait(sync, "waq")
                    P.wait(sync, "pe")
                    P.inc(sync.dma_start(
                        out=h2_sb[:],
                        in_=h2S[b, :, :, t0:t0 + 1024]
                        .rearrange("c p t -> p c t")), "lsp", 16)
                n_h2 = P.count("lsp")

                n_wg = {}
                sgm_hist = []

                @block.sync
                def _(sync):
                    for ic in range(2):
                        P.inc(sync.dma_start(out=wg_sb[ic % 2][:], in_=wgS[ic]),
                              "lsp", 16)
                        P.inc(sync.dma_start(out=wu_sb[ic % 2][:], in_=wuS[ic]),
                              "lsp", 16)
                        n_wg[ic] = P.count("lsp")

                for ic in range(NIC):
                    par = ic % 2

                    @block.tensor
                    def _(tensor, ic=ic, par=par, n_h2=n_h2):
                        P.wait(tensor, "lsp", n_wg[ic])
                        P.wait(tensor, "lsp", n_h2)
                        if len(sgm_hist) >= 2:
                            P.wait(tensor, "dv", sgm_hist[-2])
                        elif ic < 2:
                            P.wait(tensor, "dv", P.count("dv"))
                        for th in range(2):
                            for c in range(NCH):
                                nc.tensor.matmul(
                                    gate_ps[par][:, th * 512:(th + 1) * 512],
                                    wg_sb[par][:, c, :],
                                    h2_sb[:, c, th * 512:(th + 1) * 512],
                                    start=(c == 0), stop=(c == NCH - 1))
                        for th in range(2):
                            for c in range(NCH):
                                ins = nc.tensor.matmul(
                                    up_ps[par][:, th * 512:(th + 1) * 512],
                                    wu_sb[par][:, c, :],
                                    h2_sb[:, c, th * 512:(th + 1) * 512],
                                    start=(c == 0), stop=(c == NCH - 1))
                        P.inc(ins, "pe")
                    n_pe_g = P.count("pe")

                    @block.sync
                    def _(sync, ic=ic, n_pe_g=n_pe_g):
                        if ic + 2 < NIC:
                            P.wait(sync, "pe", n_pe_g)
                            P.inc(sync.dma_start(out=wg_sb[ic % 2][:],
                                                 in_=wgS[ic + 2]), "lsp", 16)
                            P.inc(sync.dma_start(out=wu_sb[ic % 2][:],
                                                 in_=wuS[ic + 2]), "lsp", 16)
                            n_wg[ic + 2] = P.count("lsp")

                    @block.scalar
                    def _(scalar, par=par, n_pe_g=n_pe_g):
                        P.wait(scalar, "pe", n_pe_g)
                        if sgm_hist:
                            P.wait(scalar, "dv", sgm_hist[-1])
                        P.inc(scalar.activation(out=sg_sb[:], in_=gate_ps[par][:],
                                                func=AF.Silu, bias=0.0, scale=1.0),
                              "ac")
                    n_ac_sg = P.count("ac")

                    @block.vector
                    def _(vector, ic=ic, par=par, n_ac_sg=n_ac_sg):
                        P.wait(vector, "ac", n_ac_sg)
                        P.inc(vector.tensor_tensor(inter_sb[:, ic, :], sg_sb[:],
                                                   up_ps[par][:], ALU.mult), "dv")
                        sgm_hist.append(P.count("dv"))

                gu.close()

                dn = ExitStack()
                wd_sb = [SB(dn, f"fwd{i}_{b}_{ftt}", [128, NIC, 128], BF16)
                         for i in range(2)]
                x1_sb = SB(dn, f"fx1_{b}_{ftt}", [128, 1024], F32)
                yo_sb = SB(dn, f"fyo_{b}_{ftt}", [128, 1024], F32)
                y_ps = [PS(dn, f"y_ps{i}_{b}_{ftt}", [128, 1024], F32)
                        for i in range(2)]
                n_inter_done = P.count("dv")
                if intD is not None and b == 0 and ftt == 0:
                    @block.scalar
                    def _(scalar):
                        P.wait(scalar, "dv", n_inter_done)
                        P.inc(scalar.dma_start(out=intD, in_=inter_sb[:]), "waq", 16)
                n_wd = {}
                yev_hist = []

                @block.sync
                def _(sync):
                    P.wait(sync, "pe")
                    P.wait(sync, "dv")
                    for dc in range(2):
                        P.inc(sync.dma_start(out=wd_sb[dc % 2][:], in_=wdS[dc]),
                              "lsp", 16)
                        n_wd[dc] = P.count("lsp")

                for dc in range(NCH):
                    par = dc % 2

                    @block.tensor
                    def _(tensor, dc=dc, par=par):
                        P.wait(tensor, "lsp", n_wd[dc])
                        P.wait(tensor, "dv", n_inter_done)
                        if len(yev_hist) >= 2:
                            P.wait(tensor, "dv", yev_hist[-2])
                        for th in range(2):
                            for cc in range(NIC):
                                ins = nc.tensor.matmul(
                                    y_ps[par][:, th * 512:(th + 1) * 512],
                                    wd_sb[par][:, cc, :],
                                    inter_sb[:, cc, th * 512:(th + 1) * 512],
                                    start=(cc == 0), stop=(cc == NIC - 1))
                        P.inc(ins, "pe")
                    n_pe_y = P.count("pe")

                    @block.sync
                    def _(sync, dc=dc, t0=t0, n_pe_y=n_pe_y):
                        if dc + 2 < NCH:
                            P.wait(sync, "pe", n_pe_y)
                            P.inc(sync.dma_start(out=wd_sb[dc % 2][:],
                                                 in_=wdS[dc + 2]), "lsp", 16)
                            n_wd[dc + 2] = P.count("lsp")
                        if yev_hist:
                            P.wait(sync, "dv", yev_hist[-1])
                        P.inc(sync.dma_start(out=x1_sb[:],
                                             in_=x1S[b, dc, :, t0:t0 + 1024]),
                              "lsp", 16)
                    n_x1l = P.count("lsp")

                    @block.vector
                    def _(vector, par=par, n_pe_y=n_pe_y, n_x1l=n_x1l):
                        P.wait(vector, "pe", n_pe_y)
                        P.wait(vector, "lsp", n_x1l)
                        P.wait(vector, "waq")
                        P.inc(vector.tensor_add(yo_sb[:], y_ps[par][:], x1_sb[:]),
                              "dv")
                        yev_hist.append(P.count("dv"))
                    n_ye = P.count("dv")

                    @block.scalar
                    def _(scalar, dc=dc, t0=t0, n_ye=n_ye):
                        P.wait(scalar, "dv", n_ye)
                        P.inc(scalar.dma_start(
                            out=outT[dc, :, tok0 + t0:tok0 + t0 + 1024],
                            in_=yo_sb[:]), "waq", 16)

                dn.close()

            ff.close()
            phb.close()
            full_barrier()

        @block.sync
        def _(sync):
            P.wait(sync, "waq")
            P.wait(sync, "sy")

    return nc



def _host_prepare(inputs):
    x = np.asarray(inputs["L0_attn_x"], dtype=np.float32)
    W_Q = np.asarray(inputs["L0_attn_W_Q"], dtype=np.float32)
    W_K = np.asarray(inputs["L0_attn_W_K"], dtype=np.float32)
    W_V = np.asarray(inputs["L0_attn_W_V"], dtype=np.float32)
    W_O = np.asarray(inputs["L0_attn_W_O"], dtype=np.float32)

    def qkvT(W):
        return np.ascontiguousarray(
            W.transpose(0, 2, 1).reshape(NH, NCH, 128, DH).transpose(0, 2, 1, 3))

    wqT = qkvT(W_Q)
    wkT = qkvT(W_K)
    wvT = qkvT(W_V)
    woT = np.ascontiguousarray(
        W_O.reshape(NH, NCH, 128, DH).transpose(1, 3, 0, 2))

    consts = dict(
        b80=_build_B80(),
        ident=np.eye(128, dtype=np.float32).astype(bfdt),
        mtri=np.triu(np.full((128, 128), -1e9, dtype=np.float32), k=1),
        ones1=np.ones((128, 1), dtype=np.float32).astype(bfdt),
        ones1f=np.ones((1, 128), dtype=np.float32),
    )
    rec = {}
    for w, (O, I), cpname in (("g", (INTER, D), "L0_ffn_gate_cp"),
                              ("u", (INTER, D), "L0_ffn_up_cp"),
                              ("d", (D, INTER), "L0_ffn_down_cp")):
        cp = np.asarray(inputs[cpname], dtype=np.float32)
        tapidx, upow, mask = _get_recon(O, I)
        taps = np.zeros(tapidx.shape[:-1] + (8,), dtype=np.float32)
        taps[..., 0:5] = cp[tapidx]
        rec[f"taps_{w}"] = taps
        rec[f"upow_{w}"] = upow
        rec[f"mask_{w}"] = mask

    per_core = []
    for c in range(NCORES):
        xs = x[BPC * c:BPC * (c + 1)]
        xTc = np.ascontiguousarray(
            xs.transpose(2, 0, 1).reshape(D, TOK)).reshape(NCH, 128, TOK)
        im = dict(xT=xTc, wqT=wqT, wkT=wkT, wvT=wvT, woT=woT, **consts, **rec)
        per_core.append(im)
    return per_core


def kernel(**inputs):
    per_core = _host_prepare(inputs)
    nc = build_program()
    res = run_bass_kernel_spmd(nc, per_core, core_ids=list(range(NCORES)))
    out = np.empty((NBATCH, SEQ, D), dtype=np.float32)
    for c in range(NCORES):
        oT = res.results[c]["outT"].reshape(D, BPC, SEQ)
        out[BPC * c:BPC * (c + 1)] = oT.transpose(1, 2, 0)
    return out


# version 16
# speedup vs baseline: 1.0632x; 1.0492x over previous
import math
from contextlib import ExitStack

import numpy as np
import ml_dtypes

import concourse.bass as bass
import concourse.mybir as mybir
from concourse.bass_utils import run_bass_kernel_spmd

F32 = mybir.dt.float32
BF16 = mybir.dt.bfloat16
U8 = mybir.dt.uint8
AF = mybir.ActivationFunctionType
ALU = mybir.AluOpType
AX = mybir.AxisListType
bfdt = ml_dtypes.bfloat16

D = 2048
SEQ = 2048
NBATCH = 16
NH = 16
DH = 128
INTER = 8192
N_CTRL = 130156
NCORES = 8
BPC = NBATCH // NCORES
TOK = BPC * SEQ
NCH = D // 128
NIC = INTER // 128
EPS = 1e-5
SCALE = 1.0 / math.sqrt(DH)
NIB = SEQ // 128
NSTRIP = SEQ // 512

NSAMP = INTER * D
DT64 = (N_CTRL - 1.0) / (NSAMP - 1.0)

CK = np.array([
    [0.0, -0.5, 1.0, -0.5],
    [1.0, 0.0, -2.5, 1.5],
    [0.0, 0.5, 2.0, -1.5],
    [0.0, 0.0, -0.5, 0.5],
], dtype=np.float64)


def _build_B80():
    p = np.arange(128, dtype=np.float64)
    out = []
    for beta in (0, 1):
        B = np.zeros((128, 20), dtype=np.float64)
        a = p * DT64 - beta
        for k in range(4):
            al = CK[k]
            g = [al[0] + al[1] * a + al[2] * a * a + al[3] * a * a * a,
                 al[1] + 2 * al[2] * a + 3 * al[3] * a * a,
                 al[2] + 3 * al[3] * a,
                 np.full(128, al[3])]
            m = k + beta
            for j in range(4):
                B[:, j * 5 + m] += g[j]
        Bh = B.astype(bfdt).astype(np.float64)
        Bl = (B - Bh).astype(np.float32).astype(bfdt).astype(np.float64)
        B80 = np.concatenate([Bh, Bh, Bl, Bl], axis=1)
        out.append(np.ascontiguousarray(B80.T).astype(np.float32).astype(bfdt))
    return np.stack(out)


def _recon_consts(out_size, in_size):
    nC = in_size // 128
    n = out_size * in_size
    t = np.linspace(0.0, N_CTRL - 1.0, n, dtype=np.float64)
    i_arr = np.clip(np.floor(t).astype(np.int64), 0, N_CTRL - 2)
    o = np.arange(out_size, dtype=np.int64)
    C = np.arange(nC, dtype=np.int64)
    n0 = o[None, :] * in_size + 128 * C[:, None]
    S = t[n0]
    I0 = i_arr[n0]
    u = S - I0
    upow = np.stack([np.ones_like(u), u, u * u, u * u * u], axis=-1).astype(np.float32)
    idx128 = n0[:, :, None] + np.arange(128)[None, None, :]
    s = (i_arr[idx128] - I0[:, :, None]).astype(np.uint8)
    tapidx = np.clip(I0[:, :, None] + np.arange(-1, 4)[None, None, :], 0, N_CTRL - 1)
    n_oc = out_size // 128
    tapidx = tapidx.transpose(1, 0, 2).reshape(n_oc, 128, nC, 5)
    upow = upow.transpose(1, 0, 2).reshape(n_oc, 128, nC, 4)
    mask = np.ascontiguousarray(s.transpose(0, 2, 1))
    return tapidx, upow, mask


_RECON_CACHE = {}


def _get_recon(out_size, in_size):
    key = (out_size, in_size)
    if key not in _RECON_CACHE:
        _RECON_CACHE[key] = _recon_consts(out_size, in_size)
    return _RECON_CACHE[key]


class Prog:
    def __init__(self, nc, es):
        self.nc = nc
        self.es = es
        self.sems = {}

    def sem(self, name):
        s = self.es.enter_context(self.nc.semaphore(name))
        self.sems[name] = [s, 0]

    def inc(self, inst, name, d=1):
        s = self.sems[name]
        inst.then_inc(s[0], d)
        s[1] += d
        return s[1]

    def count(self, name):
        return self.sems[name][1]

    def wait(self, eng, name, n=None):
        s = self.sems[name]
        n = s[1] if n is None else n
        if n > 0:
            eng.wait_ge(s[0], n)


def build_program(dbg=False):
    nc = bass.Bass("TRN2", target_bir_lowering=False)
    KIND = "ExternalOutput" if dbg else "Internal"

    xT = nc.dram_tensor("xT", [NCH, 128, TOK], F32, kind="ExternalInput").ap()
    wqT = nc.dram_tensor("wqT", [NH, 128, NCH, DH], F32, kind="ExternalInput").ap()
    wkT = nc.dram_tensor("wkT", [NH, 128, NCH, DH], F32, kind="ExternalInput").ap()
    wvT = nc.dram_tensor("wvT", [NH, 128, NCH, DH], F32, kind="ExternalInput").ap()
    woT = nc.dram_tensor("woT", [NCH, 128, NCH, 128], F32, kind="ExternalInput").ap()
    rin = {}
    for w, (O, I) in (("g", (INTER, D)), ("u", (INTER, D)), ("d", (D, INTER))):
        nC = I // 128
        n_oc = O // 128
        rin[w] = dict(
            taps=nc.dram_tensor(f"taps_{w}", [n_oc, 128, nC, 8], F32,
                                kind="ExternalInput").ap(),
            upow=nc.dram_tensor(f"upow_{w}", [n_oc, 128, nC, 4], F32,
                                kind="ExternalInput").ap(),
            mask=nc.dram_tensor(f"mask_{w}", [nC, 128, O], U8,
                                kind="ExternalInput").ap(),
            O=O, I=I, nC=nC, n_oc=n_oc,
        )
    b80_in = nc.dram_tensor("b80", [2, 80, 128], BF16, kind="ExternalInput").ap()
    ident_in = nc.dram_tensor("ident", [128, 128], BF16, kind="ExternalInput").ap()
    mtri_in = nc.dram_tensor("mtri", [128, 128], F32, kind="ExternalInput").ap()
    ones1_in = nc.dram_tensor("ones1", [128, 1], BF16, kind="ExternalInput").ap()
    ones1f_in = nc.dram_tensor("ones1f", [1, 128], F32, kind="ExternalInput").ap()

    outT = nc.dram_tensor("outT", [NCH, 128, TOK], F32, kind="ExternalOutput").ap()

    wgS = nc.dram_tensor("wgS", [NIC, 128, NCH, 128], BF16, kind=KIND).ap()
    wuS = nc.dram_tensor("wuS", [NIC, 128, NCH, 128], BF16, kind=KIND).ap()
    wdS = nc.dram_tensor("wdS", [NCH, 128, NIC, 128], BF16, kind=KIND).ap()
    x1S = nc.dram_tensor("x1S", [BPC, NCH, 128, SEQ], F32, kind=KIND).ap()
    intD = nc.dram_tensor("intD", [128, NIC, 1024], BF16, kind=KIND).ap() if dbg else None
    h2S = nc.dram_tensor("h2S", [BPC, NCH, 128, SEQ], BF16, kind=KIND).ap()
    wS = {"g": wgS, "u": wuS, "d": wdS}

    es = ExitStack()
    P = Prog(nc, es)
    for s in ["lsp", "lgp", "waq", "pe", "dv", "ac", "gp", "sy"]:
        P.sem(s)

    def SB(st, name, shape, dt_, side=None):
        return st.enter_context(nc.sbuf_tensor(name, shape, dt_, side=side))

    def PS(st, name, shape, dt_):
        return st.enter_context(nc.psum_tensor(name, shape, dt_))

    ident_sb = SB(es, "ident_sb", [128, 128], BF16)
    mtri_sb = SB(es, "mtri_sb", [128, 128], F32)
    ones1_sb = SB(es, "ones1_sb", [128, 1], BF16)
    ones1f_sb = SB(es, "ones1f_sb", [1, 128], F32)
    b80_sb = SB(es, "b80_sb", [80, 2, 128], BF16)

    with es:
        block = es.enter_context(nc.Block())

        @block.sync
        def _(sync):
            P.inc(sync.dma_start(out=ident_sb[:], in_=ident_in), "lsp", 16)
            P.inc(sync.dma_start(out=mtri_sb[:], in_=mtri_in), "lsp", 16)
            P.inc(sync.dma_start(out=ones1_sb[:], in_=ones1_in), "lsp", 16)
            P.inc(sync.dma_start(out=ones1f_sb[:], in_=ones1f_in), "lsp", 16)
            P.inc(sync.dma_start(out=b80_sb[:],
                                 in_=b80_in.rearrange("b k p -> k b p")), "lsp", 16)

        def full_barrier():
            snap = {k: P.count(k) for k in P.sems}
            for ename in ("sync", "tensor", "vector", "scalar", "gpsimd"):
                def mk(e, snap=snap):
                    for k, n in snap.items():
                        P.wait(e, k, n)
                getattr(block, ename)(mk)

        ph = ExitStack()
        taps_sb = [[SB(ph, f"taps{q}_{s}", [128, 64, 8], F32) for q in range(4)]
                   for s in range(2)]
        upow_sb = [[SB(ph, f"upow{q}_{s}", [128, 64, 4], F32) for q in range(4)]
                   for s in range(2)]
        rt_sb = [[SB(ph, f"rt{q}_{s}", [128, 64, 80], BF16) for q in range(4)]
                 for s in range(2)]
        rtf_sb = SB(ph, "rtf", [128, 64, 20], F32)
        rth_sb = SB(ph, "rth", [128, 64, 20], BF16)
        NMSK = 4
        mask_sb = [SB(ph, f"rmsk{i}", [128, 512], U8) for i in range(NMSK)]
        v1_sb = [SB(ph, f"v1s{i}", [128, 512], BF16) for i in range(2)]
        r80_sb = [SB(ph, f"r80s{i}", [80, 512], BF16) for i in range(2)]
        NOW = 4
        outw_sb = [SB(ph, f"outw{i}", [128, 512], BF16) for i in range(NOW)]
        r80_ps = [PS(ph, f"r80p{i}", [80, 512], BF16) for i in range(2)]
        v0_ps = [PS(ph, f"v0p{i}", [128, 512], F32) for i in range(2)]
        v1_ps = [PS(ph, f"v1p{i}", [128, 512], F32) for i in range(2)]

        obs = []
        for w in ("g", "u", "d"):
            for ob in range(rin[w]["O"] // 512):
                obs.append((w, ob))
        tiles = [(w, ob, C) for (w, ob) in obs for C in range(rin[w]["nC"])]
        NT = len(tiles)
        ob_of_tile = {}
        k = 0
        for oi, (w, ob) in enumerate(obs):
            for C in range(rin[w]["nC"]):
                ob_of_tile[k] = oi
                k += 1

        n_strip_load = {}
        n_rt = {}
        rT_hist = []
        rsb_hist = []
        mm_hist = []
        sel_hist = []
        wr_hist = []
        msk_hist = []
        last_T_of_ob = {}

        def emit_mask_load(t2):
            if t2 >= NT:
                return
            w2, ob2, C2 = tiles[t2]

            @block.sync
            def _(sync, w2=w2, ob2=ob2, C2=C2, t2=t2):
                if t2 >= NMSK:
                    P.wait(sync, "dv", sel_hist[t2 - NMSK])
                P.inc(sync.dma_start(
                    out=mask_sb[t2 % NMSK][:],
                    in_=rin[w2]["mask"][C2, :, ob2 * 512:(ob2 + 1) * 512]),
                    "lsp", 16)
                while len(msk_hist) <= t2:
                    msk_hist.append(0)
                msk_hist[t2] = P.count("lsp")

        def emit_strips_and_rt(oi):
            if oi >= len(obs):
                return
            w, ob = obs[oi]
            sl = oi % 2

            @block.sync
            def _(sync, w=w, ob=ob, oi=oi, sl=sl):
                if oi >= 2 and (oi - 2) in n_rt:
                    P.wait(sync, "gp", n_rt[oi - 2])
                for q in range(4):
                    oc = ob * 4 + q
                    P.inc(sync.dma_start(out=taps_sb[sl][q][:, 0:rin[w]["nC"], :],
                                         in_=rin[w]["taps"][oc]), "lsp", 16)
                    P.inc(sync.dma_start(out=upow_sb[sl][q][:, 0:rin[w]["nC"], :],
                                         in_=rin[w]["upow"][oc]), "lsp", 16)
                n_strip_load[(w, ob)] = P.count("lsp")

            @block.gpsimd
            def _(gpsimd, w=w, ob=ob, oi=oi, sl=sl):
                nC = rin[w]["nC"]
                P.wait(gpsimd, "lsp", n_strip_load[(w, ob)])
                if oi >= 2 and (oi - 2) in last_T_of_ob:
                    P.wait(gpsimd, "pe", last_T_of_ob[oi - 2])
                for q in range(4):
                    i1 = P.inc(gpsimd.tensor_tensor(
                        rtf_sb[:, 0:nC, :].rearrange("p c (j m) -> p c j m", j=4),
                        upow_sb[sl][q][:, 0:nC, :].unsqueeze(3)
                        .broadcast_to([128, nC, 4, 5]),
                        taps_sb[sl][q][:, 0:nC, 0:5].unsqueeze(2)
                        .broadcast_to([128, nC, 4, 5]),
                        ALU.mult), "gp")
                    P.wait(gpsimd, "gp", i1)
                    i2 = P.inc(gpsimd.tensor_copy(rth_sb[:, 0:nC, :],
                                                  rtf_sb[:, 0:nC, :]), "gp")
                    P.wait(gpsimd, "gp", i2)
                    P.inc(gpsimd.tensor_copy(rt_sb[sl][q][:, 0:nC, 0:20],
                                             rth_sb[:, 0:nC, :]), "gp")
                    i4 = P.inc(gpsimd.tensor_tensor(rt_sb[sl][q][:, 0:nC, 20:40],
                                                    rtf_sb[:, 0:nC, :],
                                                    rth_sb[:, 0:nC, :],
                                                    ALU.subtract), "gp")
                    P.wait(gpsimd, "gp", i4)
                    P.inc(gpsimd.tensor_copy(rt_sb[sl][q][:, 0:nC, 40:80],
                                             rt_sb[sl][q][:, 0:nC, 0:40]), "gp")
                n_rt[oi] = P.count("gp")

        def emit_stageA(t):
            w, ob, C = tiles[t]
            oi = ob_of_tile[t]
            par = t % 2
            sl = oi % 2

            @block.tensor
            def _(tensor, w=w, ob=ob, C=C, par=par, t=t, oi=oi, sl=sl):
                P.wait(tensor, "gp", n_rt[oi])
                if t >= 2:
                    P.wait(tensor, "dv", rsb_hist[t - 2])
                for q in range(4):
                    ins = nc.tensor.transpose(
                        r80_ps[par][:, q * 128:(q + 1) * 128],
                        rt_sb[sl][q][:, C, :], ident_sb[:])
                    if q == 3:
                        P.inc(ins, "pe")
                rT_hist.append(P.count("pe"))
                last_T_of_ob[oi] = rT_hist[-1]

            @block.vector
            def _(vector, par=par, t=t):
                P.wait(vector, "pe", rT_hist[t])
                if t >= 2:
                    P.wait(vector, "pe", mm_hist[t - 2])
                P.inc(vector.tensor_copy(r80_sb[par][:], r80_ps[par][:]), "dv")
                rsb_hist.append(P.count("dv"))

        def emit_stageB(t):
            w, ob, C = tiles[t]
            par = t % 2

            @block.tensor
            def _(tensor, par=par, t=t):
                P.wait(tensor, "dv", rsb_hist[t])
                if t >= 2:
                    P.wait(tensor, "dv", sel_hist[t - 2])
                nc.tensor.matmul(v0_ps[par][:], b80_sb[:, 0, :], r80_sb[par][:],
                                 start=True, stop=True)
                P.inc(nc.tensor.matmul(v1_ps[par][:], b80_sb[:, 1, :],
                                       r80_sb[par][:], start=True, stop=True),
                      "pe")
                mm_hist.append(P.count("pe"))

            @block.vector
            def _(vector, par=par, t=t):
                P.wait(vector, "pe", mm_hist[t])
                i1 = P.inc(vector.tensor_copy(v1_sb[par][:], v1_ps[par][:]), "dv")
                P.wait(vector, "lsp", msk_hist[t])
                if t >= NOW:
                    P.wait(vector, "waq", wr_hist[t - NOW])
                P.wait(vector, "dv", i1)
                P.inc(vector.select(outw_sb[t % NOW][:], mask_sb[t % NMSK][:],
                                    v1_sb[par][:], v0_ps[par][:]), "dv")
                sel_hist.append(P.count("dv"))

            @block.scalar
            def _(scalar, w=w, ob=ob, C=C, t=t):
                P.wait(scalar, "dv", sel_hist[t])
                for q in range(4):
                    oc = ob * 4 + q
                    P.inc(scalar.dma_start(
                        out=wS[w][oc, :, C, :],
                        in_=outw_sb[t % NOW][:, q * 128:(q + 1) * 128]),
                        "waq", 16)
                wr_hist.append(P.count("waq"))

        for t2 in range(min(NMSK, NT)):
            emit_mask_load(t2)
        emit_strips_and_rt(0)
        emit_strips_and_rt(1)
        for t in range(NT):
            oi = ob_of_tile[t]
            emit_stageA(t)
            if t + 1 < NT and ob_of_tile[t + 1] == oi + 1:
                emit_strips_and_rt(oi + 2)
            if t >= 1:
                emit_stageB(t - 1)
                emit_mask_load(t - 1 + NMSK)
        emit_stageB(NT - 1)
        emit_mask_load(NT - 1 + NMSK)

        ph.close()
        full_barrier()

        for b in range(BPC):
            tok0 = b * SEQ

            phb = ExitStack()
            h1s = ExitStack()
            h1T_sb = SB(h1s, f"h1T_{b}", [128, NCH, SEQ], BF16)
            zs = ExitStack()
            zall_sb = SB(zs, f"zall_{b}", [128, NH, SEQ], BF16, side="right")

            ln = ExitStack()
            x_sb = SB(ln, f"lx_{b}", [128, NCH, 512], F32)
            xb_sb = [SB(ln, f"lxb{i}_{b}", [128, 512], BF16) for i in range(2)]
            sq_sb = [SB(ln, f"lsq{i}_{b}", [128, 512], BF16) for i in range(2)]
            m_sb = SB(ln, f"lm_{b}", [1, 512], F32)
            var_sb = SB(ln, f"lvar_{b}", [1, 512], F32)
            r_sb = SB(ln, f"lr_{b}", [1, 512], F32)
            mr_sb = SB(ln, f"lmr_{b}", [1, 512], F32)
            rb_sb = SB(ln, f"lrb_{b}", [128, 512], F32)
            mrb_sb = SB(ln, f"lmrb_{b}", [128, 512], F32)
            t1_sb = SB(ln, f"lt1_{b}", [128, 512], F32)
            st_ps = PS(ln, f"lst_{b}", [1, 1024], F32)
            bc_ps = PS(ln, f"lbc_{b}", [128, 1024], F32)

            cast_hist = []
            stat_hist = []

            for tt in range(NSTRIP):
                t0 = tt * 512

                @block.sync
                def _(sync, t0=t0, tt=tt):
                    P.wait(sync, "gp")
                    if tt == 0:
                        P.wait(sync, "pe", P.count("pe"))
                        P.wait(sync, "waq")
                        P.wait(sync, "dv", P.count("dv"))
                    P.inc(sync.dma_start(
                        out=x_sb[:],
                        in_=xT[:, :, tok0 + t0:tok0 + t0 + 512]
                        .rearrange("c p t -> p c t")), "lsp", 16)
                n_x = P.count("lsp")

                for c in range(NCH):
                    cp = c % 2

                    @block.vector
                    def _(vector, c=c, cp=cp, n_x=n_x):
                        P.wait(vector, "lsp", n_x)
                        if len(stat_hist) >= 2:
                            P.wait(vector, "pe", stat_hist[-2][1])
                        P.inc(vector.tensor_copy(xb_sb[cp][:], x_sb[:, c, :]), "dv")

                    @block.gpsimd
                    def _(gpsimd, c=c, cp=cp, n_x=n_x):
                        P.wait(gpsimd, "lsp", n_x)
                        if len(stat_hist) >= 2:
                            P.wait(gpsimd, "pe", stat_hist[-2][1])
                        P.inc(gpsimd.tensor_tensor(sq_sb[cp][:], x_sb[:, c, :],
                                                   x_sb[:, c, :], ALU.mult), "gp")

                    @block.tensor
                    def _(tensor, c=c, cp=cp):
                        P.wait(tensor, "dv", P.count("dv"))
                        P.wait(tensor, "gp", P.count("gp"))
                        nc.tensor.matmul(st_ps[:, 0:512], ones1_sb[:], xb_sb[cp][:],
                                         start=(c == 0), stop=(c == NCH - 1))
                        ins = nc.tensor.matmul(st_ps[:, 512:1024], ones1_sb[:],
                                               sq_sb[cp][:], start=(c == 0),
                                               stop=(c == NCH - 1))
                        P.inc(ins, "pe")
                        stat_hist.append((c, P.count("pe")))
                n_pe_st = P.count("pe")

                @block.scalar
                def _(scalar, n_pe_st=n_pe_st):
                    P.wait(scalar, "pe", n_pe_st)
                    P.inc(scalar.mul(m_sb[:], st_ps[:, 0:512], 1.0 / D), "ac")
                    P.inc(scalar.mul(var_sb[:], st_ps[:, 512:1024], 1.0 / D), "ac")
                n_ac_m = P.count("ac")

                @block.vector
                def _(vector, n_ac_m=n_ac_m):
                    P.wait(vector, "ac", n_ac_m)
                    i1 = P.inc(vector.tensor_tensor(r_sb[:], m_sb[:], m_sb[:],
                                                    ALU.mult), "dv")
                    P.wait(vector, "dv", i1)
                    i2 = P.inc(vector.tensor_sub(var_sb[:], var_sb[:], r_sb[:]),
                               "dv")
                    P.wait(vector, "dv", i2)
                    P.inc(vector.tensor_scalar_add(var_sb[:], var_sb[:], EPS), "dv")
                n_dv_v = P.count("dv")

                @block.scalar
                def _(scalar, n_dv_v=n_dv_v):
                    P.wait(scalar, "dv", n_dv_v)
                    i1 = P.inc(scalar.activation(out=var_sb[:], in_=var_sb[:],
                                                 func=AF.Ln, bias=0.0, scale=1.0),
                               "ac")
                    P.wait(scalar, "ac", i1)
                    P.inc(scalar.activation(out=r_sb[:], in_=var_sb[:], func=AF.Exp,
                                            bias=0.0, scale=-0.5), "ac")
                n_ac_r = P.count("ac")

                @block.vector
                def _(vector, n_ac_r=n_ac_r):
                    P.wait(vector, "ac", n_ac_r)
                    P.inc(vector.tensor_tensor(mr_sb[:], m_sb[:], r_sb[:], ALU.mult),
                          "dv")
                n_dv_mr = P.count("dv")

                @block.tensor
                def _(tensor, n_dv_mr=n_dv_mr):
                    P.wait(tensor, "dv", n_dv_mr)
                    nc.tensor.matmul(bc_ps[:, 0:512], ones1f_sb[:], r_sb[:],
                                     start=True, stop=True)
                    P.inc(nc.tensor.matmul(bc_ps[:, 512:1024], ones1f_sb[:],
                                           mr_sb[:], start=True, stop=True), "pe")
                n_pe_bc = P.count("pe")

                @block.vector
                def _(vector, n_pe_bc=n_pe_bc):
                    P.wait(vector, "pe", n_pe_bc)
                    P.inc(vector.tensor_copy(rb_sb[:], bc_ps[:, 0:512]), "dv")
                    P.inc(vector.tensor_copy(mrb_sb[:], bc_ps[:, 512:1024]), "dv")
                n_dv_bc = P.count("dv")

                @block.gpsimd
                def _(gpsimd, t0=t0, n_dv_bc=n_dv_bc):
                    P.wait(gpsimd, "dv", n_dv_bc)
                    for c in range(NCH):
                        i1 = P.inc(gpsimd.tensor_tensor(t1_sb[:], x_sb[:, c, :],
                                                        rb_sb[:], ALU.mult), "gp")
                        P.wait(gpsimd, "gp", i1)
                        i2 = P.inc(gpsimd.tensor_tensor(
                            h1T_sb[:, c, t0:t0 + 512], t1_sb[:], mrb_sb[:],
                            ALU.subtract), "gp")
                        P.wait(gpsimd, "gp", i2)

            ln.close()

            at = ExitStack()
            wq_sb = [SB(at, f"wq{i}_{b}", [128, NCH, 128], BF16) for i in range(2)]
            wk_sb = [SB(at, f"wk{i}_{b}", [128, NCH, 128], BF16) for i in range(2)]
            wv_sb = [SB(at, f"wv{i}_{b}", [128, NCH, 128], BF16) for i in range(2)]
            qT_sb = SB(at, f"qT_{b}", [128, SEQ], BF16)
            kT_sb = SB(at, f"kT_{b}", [128, SEQ], BF16)
            v_sb = SB(at, f"v_{b}", [128, NIB, 128], BF16)
            scsb = [SB(at, f"scsb{i}_{b}", [128, SEQ], F32) for i in range(2)]
            p_sb = SB(at, f"p_{b}", [128, SEQ], BF16)
            p2_sb = SB(at, f"p2_{b}", [128, SEQ], BF16)
            pt_sb = SB(at, f"pt_{b}", [128, NIB, 512], BF16)
            nm_sb = SB(at, f"nm_{b}", [128, 1], F32)
            sum_sb = SB(at, f"sum_{b}", [128, 1], F32)
            rc_sb = SB(at, f"rc_{b}", [128, 1], F32)

            wld_hist = {}

            def emit_wload(h):
                par = h % 2

                @block.gpsimd
                def _(gpsimd, h=h, par=par):
                    if h >= 2:
                        P.wait(gpsimd, "pe", qkv_hist[h - 2])
                    for win, dst in ((wqT, wq_sb), (wkT, wk_sb), (wvT, wv_sb)):
                        P.inc(gpsimd.dma_start(out=dst[par][:], in_=win[h]),
                              "lgp", 16)
                    wld_hist[h] = P.count("lgp")

            qkv_hist = {}

            emit_wload(0)
            emit_wload(1)

            for h in range(NH):
                par = h % 2

                qk = ExitStack()
                q_ps = [PS(qk, f"q_ps{i}_{b}_{h}", [128, 512], F32)
                        for i in range(2)]
                k_ps = [PS(qk, f"k_ps{i}_{b}_{h}", [128, 512], F32)
                        for i in range(2)]
                v_ps = [PS(qk, f"v_ps{i}_{b}_{h}", [128, 128], F32)
                        for i in range(2)]
                qkev_hist = []
                vev_hist = []

                @block.tensor
                def _(tensor, h=h, par=par):
                    P.wait(tensor, "dv", P.count("dv"))
                    P.wait(tensor, "ac", P.count("ac"))
                    P.wait(tensor, "gp", P.count("gp"))
                    P.wait(tensor, "lgp", wld_hist[h])
                    for tt in range(NSTRIP):
                        qp = tt % 2
                        if tt > 1:
                            P.wait(tensor, "ac", qkev_hist[tt - 2])
                        for c in range(NCH):
                            nc.tensor.matmul(q_ps[qp][:], wq_sb[par][:, c, :],
                                             h1T_sb[:, c, tt * 512:(tt + 1) * 512],
                                             start=(c == 0), stop=(c == NCH - 1))
                        for c in range(NCH):
                            ins = nc.tensor.matmul(
                                k_ps[qp][:], wk_sb[par][:, c, :],
                                h1T_sb[:, c, tt * 512:(tt + 1) * 512],
                                start=(c == 0), stop=(c == NCH - 1))
                        npe = P.inc(ins, "pe")

                        @block.scalar
                        def _(scalar, h=h, tt=tt, qp=qp, npe=npe):
                            P.wait(scalar, "pe", npe)
                            scalar.copy(qT_sb[:, tt * 512:(tt + 1) * 512],
                                        q_ps[qp][:])
                            P.inc(scalar.copy(kT_sb[:, tt * 512:(tt + 1) * 512],
                                              k_ps[qp][:]), "ac")
                            qkev_hist.append(P.count("ac"))

                    for jb in range(NIB):
                        vp = jb % 2
                        if jb >= 2:
                            P.wait(tensor, "dv", vev_hist[jb - 2])
                        for c in range(NCH):
                            ins = nc.tensor.matmul(
                                v_ps[vp][:], h1T_sb[:, c, jb * 128:(jb + 1) * 128],
                                wv_sb[par][:, c, :],
                                start=(c == 0), stop=(c == NCH - 1))
                        npe = P.inc(ins, "pe")

                        @block.vector
                        def _(vector, jb=jb, vp=vp, npe=npe):
                            P.wait(vector, "pe", npe)
                            P.inc(vector.tensor_copy(v_sb[:, jb, :], v_ps[vp][:]),
                                  "dv")
                            vev_hist.append(P.count("dv"))
                    qkv_hist[h] = P.count("pe")

                if h + 2 < NH:
                    emit_wload(h + 2)

                qk.close()

                av = ExitStack()
                scpp = [PS(av, f"scpp{i}_{b}_{h}", [128, 1024], F32)
                        for i in range(2)]
                pt_ps = [PS(av, f"pt_ps{i}_{b}_{h}", [128, 128], BF16)
                         for i in range(2)]
                z_ps = PS(av, f"z_ps_{b}_{h}", [128, 512], F32)

                n_qk_last = qkev_hist[-1]
                cp_hist = []
                exp_hist = {}
                p2_hist = {}
                ptflat = []

                def emit_scores(h, ib):
                    jext = (ib + 1) * 128
                    par2 = ib % 2

                    @block.tensor
                    def _(tensor, h=h, ib=ib, jext=jext, par2=par2):
                        if ib == 0:
                            P.wait(tensor, "ac", n_qk_last)
                            P.wait(tensor, "dv", P.count("dv"))
                        if (ib - 2) in exp_hist:
                            P.wait(tensor, "ac", exp_hist[ib - 2])
                        j0 = 0
                        while j0 < jext:
                            jn = min(1024, jext - j0)
                            pp = len(cp_hist) % 2
                            if len(cp_hist) >= 2:
                                P.wait(tensor, "ac", cp_hist[-2])
                            ins = nc.tensor.matmul(
                                scpp[pp][:, 0:min(jn, 512)],
                                qT_sb[:, ib * 128:(ib + 1) * 128],
                                kT_sb[:, j0:j0 + min(jn, 512)],
                                start=True, stop=True)
                            if jn > 512:
                                ins = nc.tensor.matmul(
                                    scpp[pp][:, 512:jn],
                                    qT_sb[:, ib * 128:(ib + 1) * 128],
                                    kT_sb[:, j0 + 512:j0 + jn],
                                    start=True, stop=True)
                            npe = P.inc(ins, "pe")

                            @block.scalar
                            def _(scalar, j0=j0, jn=jn, npe=npe, par2=par2,
                                  pp=pp):
                                P.wait(scalar, "pe", npe)
                                P.inc(scalar.copy(scsb[par2][:, j0:j0 + jn],
                                                  scpp[pp][:, 0:jn]), "ac")
                                cp_hist.append(P.count("ac"))
                            j0 += jn

                def emit_softmax(h, ib):
                    jext = (ib + 1) * 128
                    par2 = ib % 2
                    n_cp = cp_hist[-1]

                    @block.vector
                    def _(vector, ib=ib, jext=jext, par2=par2, n_cp=n_cp):
                        P.wait(vector, "ac", n_cp)
                        if (ib - 1) in exp_hist:
                            P.wait(vector, "ac", exp_hist[ib - 1])
                        i1 = P.inc(vector.tensor_add(
                            scsb[par2][:, jext - 128:jext],
                            scsb[par2][:, jext - 128:jext], mtri_sb[:]), "dv")
                        P.wait(vector, "dv", i1)
                        i2 = P.inc(vector.reduce_max(nm_sb[:],
                                                     scsb[par2][:, 0:jext],
                                                     axis=AX.X, negate=True), "dv")
                        P.wait(vector, "dv", i2)
                        P.inc(vector.tensor_scalar_mul(nm_sb[:], nm_sb[:], SCALE),
                              "dv")
                    n_dv_nm = P.count("dv")

                    @block.scalar
                    def _(scalar, ib=ib, jext=jext, par2=par2, n_dv_nm=n_dv_nm):
                        P.wait(scalar, "dv", n_dv_nm)
                        if (ib - 1) in p2_hist:
                            P.wait(scalar, "dv", p2_hist[ib - 1])
                        P.inc(scalar.activation(
                            out=p_sb[:, 0:jext], in_=scsb[par2][:, 0:jext],
                            func=AF.Exp, bias=nm_sb[:], scale=SCALE,
                            accum_out=sum_sb[:]), "ac")
                        exp_hist[ib] = P.count("ac")

                    npe_now = P.count("pe")

                    @block.vector
                    def _(vector, ib=ib, jext=jext, npe_now=npe_now):
                        P.wait(vector, "ac", exp_hist[ib])
                        i1 = P.inc(vector.reciprocal(rc_sb[:], sum_sb[:]), "dv")
                        P.wait(vector, "pe", npe_now)
                        P.wait(vector, "dv", i1)
                        P.inc(vector.tensor_scalar_mul(p2_sb[:, 0:jext],
                                                       p_sb[:, 0:jext], rc_sb[:]),
                              "dv")
                        p2_hist[ib] = P.count("dv")

                def emit_transposes(h, ib, ibr):
                    for jb in range(ib + 1):
                        tp = len(ptflat) % 2

                        @block.tensor
                        def _(tensor, ib=ib, jb=jb, tp=tp):
                            P.wait(tensor, "dv", p2_hist[ib])
                            if len(ptflat) >= 2:
                                P.wait(tensor, "dv", ptflat[-2])
                            P.inc(nc.tensor.transpose(
                                pt_ps[tp][:], p2_sb[:, jb * 128:(jb + 1) * 128],
                                ident_sb[:]), "pe")
                        npe = P.count("pe")

                        @block.vector
                        def _(vector, ib=ib, jb=jb, tp=tp, ibr=ibr, npe=npe):
                            P.wait(vector, "pe", npe)
                            P.inc(vector.tensor_copy(
                                pt_sb[:, jb, ibr * 128:(ibr + 1) * 128],
                                pt_ps[tp][:]), "dv")
                            ptflat.append(P.count("dv"))

                emit_scores(h, 0)
                for istrip in range(NSTRIP):
                    ib0 = istrip * 4
                    for ibr in range(4):
                        ib = ib0 + ibr
                        emit_softmax(h, ib)
                        if ib + 1 < NIB:
                            emit_scores(h, ib + 1)
                        emit_transposes(h, ib, ibr)

                    n_pt_last = P.count("dv")

                    @block.tensor
                    def _(tensor, ib0=ib0, n_pt_last=n_pt_last):
                        P.wait(tensor, "dv", n_pt_last)
                        njb = ib0 + 4
                        for jb in range(njb):
                            rel0 = max(0, jb - ib0) * 128
                            ins = nc.tensor.matmul(
                                z_ps[:, rel0:512], v_sb[:, jb, :],
                                pt_sb[:, jb, rel0:512],
                                start=(jb == 0), stop=(jb == njb - 1))
                        P.inc(ins, "pe")
                    n_pe_z = P.count("pe")

                    @block.vector
                    def _(vector, h=h, istrip=istrip, n_pe_z=n_pe_z):
                        P.wait(vector, "pe", n_pe_z)
                        P.inc(vector.tensor_copy(
                            zall_sb[:, h, istrip * 512:(istrip + 1) * 512],
                            z_ps[:]), "dv")

                av.close()

            at.close()
            h1s.close()

            op = ExitStack()
            wo_sb = SB(op, f"wo_{b}", [128, NCH, NCH, 128], BF16)
            wtmp2_sb = SB(op, f"wtmp2_{b}", [128, 2048], F32)
            x_sb = SB(op, f"ox_{b}", [128, NCH, 512], F32)
            xb_sb = [SB(op, f"oxb{i}_{b}", [128, 512], BF16) for i in range(2)]
            sq_sb = [SB(op, f"osq{i}_{b}", [128, 512], BF16) for i in range(2)]
            m_sb = SB(op, f"om_{b}", [1, 512], F32)
            var_sb = SB(op, f"ovar_{b}", [1, 512], F32)
            r_sb = SB(op, f"or_{b}", [1, 512], F32)
            mr_sb = SB(op, f"omr_{b}", [1, 512], F32)
            rb_sb = SB(op, f"orb_{b}", [128, 512], F32)
            mrb_sb = SB(op, f"omrb_{b}", [128, 512], F32)
            t1_sb = SB(op, f"ot1_{b}", [128, 512], F32)
            h2full_sb = SB(op, f"oh2f_{b}", [128, NCH, 512], BF16)
            ao_ps = [PS(op, f"ao_ps{i}_{b}", [128, 512], F32) for i in range(2)]
            st_ps = PS(op, f"ost_{b}", [1, 1024], F32)
            bc_ps = PS(op, f"obc_{b}", [128, 1024], F32)

            for wc in range(NCH):
                @block.sync
                def _(sync, wc=wc):
                    P.wait(sync, "dv", P.count("dv"))
                    if wc == 0:
                        P.wait(sync, "pe", P.count("pe"))
                        P.wait(sync, "ac", P.count("ac"))
                    P.inc(sync.dma_start(out=wtmp2_sb[:],
                                         in_=woT[wc].rearrange("p a b -> p (a b)")),
                          "lsp", 16)
                nwl = P.count("lsp")

                @block.vector
                def _(vector, wc=wc, nwl=nwl):
                    P.wait(vector, "lsp", nwl)
                    P.inc(vector.tensor_copy(
                        wo_sb[:, wc].rearrange("p a b -> p (a b)"), wtmp2_sb[:]),
                        "dv")
            n_wo = P.count("dv")

            x1_hist = []
            n_x1wr = [0]
            for tt in range(NSTRIP):
                t0 = tt * 512
                sq_hist = []

                for dc in range(NCH):
                    par = dc % 2
                    cp = dc % 2

                    @block.sync
                    def _(sync, dc=dc, t0=t0):
                        P.wait(sync, "gp")
                        P.wait(sync, "waq", n_x1wr[0])
                        P.inc(sync.dma_start(
                            out=x_sb[:, dc, :],
                            in_=xT[dc, :, tok0 + t0:tok0 + t0 + 512]), "lsp", 16)
                    n_x = P.count("lsp")

                    @block.tensor
                    def _(tensor, dc=dc, par=par, t0=t0, n_wo=n_wo):
                        P.wait(tensor, "dv", n_wo)
                        if not x1_hist:
                            P.wait(tensor, "ac", P.count("ac"))
                        if len(x1_hist) >= 2:
                            P.wait(tensor, "dv", x1_hist[-2])
                        for hh in range(NH):
                            ins = nc.tensor.matmul(
                                ao_ps[par][:], wo_sb[:, dc, hh, :],
                                zall_sb[:, hh, t0:t0 + 512],
                                start=(hh == 0), stop=(hh == NH - 1))
                        P.inc(ins, "pe")
                    n_pe_ao = P.count("pe")

                    @block.vector
                    def _(vector, dc=dc, par=par, n_pe_ao=n_pe_ao, n_x=n_x):
                        P.wait(vector, "pe", n_pe_ao)
                        P.wait(vector, "lsp", n_x)
                        P.inc(vector.tensor_add(x_sb[:, dc, :], ao_ps[par][:],
                                                x_sb[:, dc, :]), "dv")
                        x1_hist.append(P.count("dv"))
                    n_x1 = P.count("dv")

                    @block.vector
                    def _(vector, dc=dc, cp=cp, n_x1=n_x1):
                        if len(sq_hist) >= 2:
                            P.wait(vector, "pe", sq_hist[-2][1])
                        P.inc(vector.tensor_copy(xb_sb[cp][:], x_sb[:, dc, :]),
                              "dv")

                    @block.gpsimd
                    def _(gpsimd, dc=dc, cp=cp, n_x1=n_x1):
                        P.wait(gpsimd, "dv", n_x1)
                        if len(sq_hist) >= 2:
                            P.wait(gpsimd, "pe", sq_hist[-2][1])
                        P.inc(gpsimd.tensor_tensor(sq_sb[cp][:], x_sb[:, dc, :],
                                                   x_sb[:, dc, :], ALU.mult), "gp")
                    n_cast = (P.count("dv"), P.count("gp"))

                    @block.tensor
                    def _(tensor, dc=dc, cp=cp, n_cast=n_cast):
                        P.wait(tensor, "dv", n_cast[0])
                        P.wait(tensor, "gp", n_cast[1])
                        nc.tensor.matmul(st_ps[:, 0:512], ones1_sb[:], xb_sb[cp][:],
                                         start=(dc == 0), stop=(dc == NCH - 1))
                        ins = nc.tensor.matmul(st_ps[:, 512:1024], ones1_sb[:],
                                               sq_sb[cp][:], start=(dc == 0),
                                               stop=(dc == NCH - 1))
                        P.inc(ins, "pe")
                        sq_hist.append((dc, P.count("pe")))
                n_pe_st = P.count("pe")

                @block.scalar
                def _(scalar, n_pe_st=n_pe_st):
                    P.wait(scalar, "pe", n_pe_st)
                    P.inc(scalar.mul(m_sb[:], st_ps[:, 0:512], 1.0 / D), "ac")
                    P.inc(scalar.mul(var_sb[:], st_ps[:, 512:1024], 1.0 / D), "ac")
                n_ac_m = P.count("ac")

                @block.vector
                def _(vector, n_ac_m=n_ac_m):
                    P.wait(vector, "ac", n_ac_m)
                    i1 = P.inc(vector.tensor_tensor(r_sb[:], m_sb[:], m_sb[:],
                                                    ALU.mult), "dv")
                    P.wait(vector, "dv", i1)
                    i2 = P.inc(vector.tensor_sub(var_sb[:], var_sb[:], r_sb[:]),
                               "dv")
                    P.wait(vector, "dv", i2)
                    P.inc(vector.tensor_scalar_add(var_sb[:], var_sb[:], EPS), "dv")
                n_dv_v = P.count("dv")

                @block.scalar
                def _(scalar, n_dv_v=n_dv_v):
                    P.wait(scalar, "dv", n_dv_v)
                    i1 = P.inc(scalar.activation(out=var_sb[:], in_=var_sb[:],
                                                 func=AF.Ln, bias=0.0, scale=1.0),
                               "ac")
                    P.wait(scalar, "ac", i1)
                    P.inc(scalar.activation(out=r_sb[:], in_=var_sb[:], func=AF.Exp,
                                            bias=0.0, scale=-0.5), "ac")
                n_ac_r = P.count("ac")

                @block.vector
                def _(vector, n_ac_r=n_ac_r):
                    P.wait(vector, "ac", n_ac_r)
                    P.inc(vector.tensor_tensor(mr_sb[:], m_sb[:], r_sb[:], ALU.mult),
                          "dv")
                n_dv_mr = P.count("dv")

                @block.tensor
                def _(tensor, n_dv_mr=n_dv_mr):
                    P.wait(tensor, "dv", n_dv_mr)
                    nc.tensor.matmul(bc_ps[:, 0:512], ones1f_sb[:], r_sb[:],
                                     start=True, stop=True)
                    P.inc(nc.tensor.matmul(bc_ps[:, 512:1024], ones1f_sb[:],
                                           mr_sb[:], start=True, stop=True), "pe")
                n_pe_bc = P.count("pe")

                @block.vector
                def _(vector, n_pe_bc=n_pe_bc):
                    P.wait(vector, "pe", n_pe_bc)
                    P.inc(vector.tensor_copy(rb_sb[:], bc_ps[:, 0:512]), "dv")
                    P.inc(vector.tensor_copy(mrb_sb[:], bc_ps[:, 512:1024]), "dv")
                n_dv_bc = P.count("dv")

                n_x1_all = P.count("dv")

                @block.scalar
                def _(scalar, t0=t0, n_x1_all=n_x1_all):
                    P.wait(scalar, "dv", n_x1_all)
                    P.inc(scalar.dma_start(
                        out=x1S[b, :, :, t0:t0 + 512].rearrange("c p t -> p c t"),
                        in_=x_sb[:]), "waq", 16)
                n_x1wr[0] = P.count("waq")

                @block.gpsimd
                def _(gpsimd, t0=t0, n_dv_bc=n_dv_bc):
                    P.wait(gpsimd, "dv", n_dv_bc)
                    for c in range(NCH):
                        i1 = P.inc(gpsimd.tensor_tensor(t1_sb[:], x_sb[:, c, :],
                                                        rb_sb[:], ALU.mult), "gp")
                        P.wait(gpsimd, "gp", i1)
                        i2 = P.inc(gpsimd.tensor_tensor(h2full_sb[:, c, :],
                                                        t1_sb[:],
                                                        mrb_sb[:], ALU.subtract),
                                   "gp")
                        P.wait(gpsimd, "gp", i2)
                ngp_all = P.count("gp")

                @block.scalar
                def _(scalar, t0=t0, ngp_all=ngp_all):
                    P.wait(scalar, "gp", ngp_all)
                    P.inc(scalar.dma_start(
                        out=h2S[b, :, :, t0:t0 + 512].rearrange("c p t -> p c t"),
                        in_=h2full_sb[:]), "waq", 16)
                n_x1wr[0] = P.count("waq")

            op.close()
            zs.close()

            ff = ExitStack()
            inter_sb = SB(ff, f"fint_{b}", [128, NIC, 1024], BF16)
            wg_sb = [SB(ff, f"fwg{i}_{b}", [128, NCH, 128], BF16) for i in range(2)]
            wu_sb = [SB(ff, f"fwu{i}_{b}", [128, NCH, 128], BF16) for i in range(2)]
            sg_sb = SB(ff, f"fsg_{b}", [128, 1024], BF16)

            for ftt in range(BPC):
                t0 = ftt * 1024

                gu = ExitStack()
                h2_sb = SB(gu, f"fh2_{b}_{ftt}", [128, NCH, 1024], BF16)
                gate_ps = [PS(gu, f"g_ps{i}_{b}_{ftt}", [128, 1024], F32)
                           for i in range(2)]
                up_ps = [PS(gu, f"u_ps{i}_{b}_{ftt}", [128, 1024], F32)
                         for i in range(2)]

                @block.sync
                def _(sync, t0=t0):
                    P.wait(sync, "waq")
                    P.wait(sync, "pe")
                    P.inc(sync.dma_start(
                        out=h2_sb[:],
                        in_=h2S[b, :, :, t0:t0 + 1024]
                        .rearrange("c p t -> p c t")), "lsp", 16)
                n_h2 = P.count("lsp")

                n_wg = {}
                sgm_hist = []

                @block.sync
                def _(sync):
                    P.wait(sync, "waq")
                    P.wait(sync, "pe", P.count("pe"))
                    for ic in range(2):
                        P.inc(sync.dma_start(out=wg_sb[ic % 2][:], in_=wgS[ic]),
                              "lsp", 16)
                        P.inc(sync.dma_start(out=wu_sb[ic % 2][:], in_=wuS[ic]),
                              "lsp", 16)
                        n_wg[ic] = P.count("lsp")

                for ic in range(NIC):
                    par = ic % 2

                    @block.tensor
                    def _(tensor, ic=ic, par=par, n_h2=n_h2):
                        P.wait(tensor, "lsp", n_wg[ic])
                        P.wait(tensor, "lsp", n_h2)
                        if len(sgm_hist) >= 2:
                            P.wait(tensor, "dv", sgm_hist[-2])
                        elif ic < 2:
                            P.wait(tensor, "dv", P.count("dv"))
                            P.wait(tensor, "ac", P.count("ac"))
                        for th in range(2):
                            for c in range(NCH):
                                nc.tensor.matmul(
                                    gate_ps[par][:, th * 512:(th + 1) * 512],
                                    wg_sb[par][:, c, :],
                                    h2_sb[:, c, th * 512:(th + 1) * 512],
                                    start=(c == 0), stop=(c == NCH - 1))
                        for th in range(2):
                            for c in range(NCH):
                                ins = nc.tensor.matmul(
                                    up_ps[par][:, th * 512:(th + 1) * 512],
                                    wu_sb[par][:, c, :],
                                    h2_sb[:, c, th * 512:(th + 1) * 512],
                                    start=(c == 0), stop=(c == NCH - 1))
                        P.inc(ins, "pe")
                    n_pe_g = P.count("pe")

                    @block.sync
                    def _(sync, ic=ic, n_pe_g=n_pe_g):
                        if ic + 2 < NIC:
                            P.wait(sync, "pe", n_pe_g)
                            P.inc(sync.dma_start(out=wg_sb[ic % 2][:],
                                                 in_=wgS[ic + 2]), "lsp", 16)
                            P.inc(sync.dma_start(out=wu_sb[ic % 2][:],
                                                 in_=wuS[ic + 2]), "lsp", 16)
                            n_wg[ic + 2] = P.count("lsp")

                    @block.scalar
                    def _(scalar, par=par, n_pe_g=n_pe_g):
                        P.wait(scalar, "pe", n_pe_g)
                        if sgm_hist:
                            P.wait(scalar, "dv", sgm_hist[-1])
                        P.inc(scalar.activation(out=sg_sb[:], in_=gate_ps[par][:],
                                                func=AF.Silu, bias=0.0, scale=1.0),
                              "ac")
                    n_ac_sg = P.count("ac")

                    @block.vector
                    def _(vector, ic=ic, par=par, n_ac_sg=n_ac_sg):
                        P.wait(vector, "ac", n_ac_sg)
                        P.inc(vector.tensor_tensor(inter_sb[:, ic, :], sg_sb[:],
                                                   up_ps[par][:], ALU.mult), "dv")
                        sgm_hist.append(P.count("dv"))

                gu.close()

                dn = ExitStack()
                wd_sb = [SB(dn, f"fwd{i}_{b}_{ftt}", [128, NIC, 128], BF16)
                         for i in range(2)]
                x1_sb = SB(dn, f"fx1_{b}_{ftt}", [128, 1024], F32)
                yo_sb = SB(dn, f"fyo_{b}_{ftt}", [128, 1024], F32)
                y_ps = [PS(dn, f"y_ps{i}_{b}_{ftt}", [128, 1024], F32)
                        for i in range(2)]
                n_inter_done = P.count("dv")
                if intD is not None and b == 0 and ftt == 0:
                    @block.scalar
                    def _(scalar):
                        P.wait(scalar, "dv", n_inter_done)
                        P.inc(scalar.dma_start(out=intD, in_=inter_sb[:]), "waq", 16)
                n_wd = {}
                yev_hist = []

                @block.sync
                def _(sync):
                    P.wait(sync, "pe")
                    P.wait(sync, "dv")
                    for dc in range(2):
                        P.inc(sync.dma_start(out=wd_sb[dc % 2][:], in_=wdS[dc]),
                              "lsp", 16)
                        n_wd[dc] = P.count("lsp")

                for dc in range(NCH):
                    par = dc % 2

                    @block.tensor
                    def _(tensor, dc=dc, par=par):
                        P.wait(tensor, "lsp", n_wd[dc])
                        P.wait(tensor, "dv", n_inter_done)
                        if len(yev_hist) >= 2:
                            P.wait(tensor, "dv", yev_hist[-2])
                        for th in range(2):
                            for cc in range(NIC):
                                ins = nc.tensor.matmul(
                                    y_ps[par][:, th * 512:(th + 1) * 512],
                                    wd_sb[par][:, cc, :],
                                    inter_sb[:, cc, th * 512:(th + 1) * 512],
                                    start=(cc == 0), stop=(cc == NIC - 1))
                        P.inc(ins, "pe")
                    n_pe_y = P.count("pe")

                    @block.sync
                    def _(sync, dc=dc, t0=t0, n_pe_y=n_pe_y):
                        if dc + 2 < NCH:
                            P.wait(sync, "pe", n_pe_y)
                            P.inc(sync.dma_start(out=wd_sb[dc % 2][:],
                                                 in_=wdS[dc + 2]), "lsp", 16)
                            n_wd[dc + 2] = P.count("lsp")
                        if yev_hist:
                            P.wait(sync, "dv", yev_hist[-1])
                        P.inc(sync.dma_start(out=x1_sb[:],
                                             in_=x1S[b, dc, :, t0:t0 + 1024]),
                              "lsp", 16)
                    n_x1l = P.count("lsp")

                    @block.vector
                    def _(vector, par=par, n_pe_y=n_pe_y, n_x1l=n_x1l):
                        P.wait(vector, "pe", n_pe_y)
                        P.wait(vector, "lsp", n_x1l)
                        P.wait(vector, "waq")
                        P.inc(vector.tensor_add(yo_sb[:], y_ps[par][:], x1_sb[:]),
                              "dv")
                        yev_hist.append(P.count("dv"))
                    n_ye = P.count("dv")

                    @block.scalar
                    def _(scalar, dc=dc, t0=t0, n_ye=n_ye):
                        P.wait(scalar, "dv", n_ye)
                        P.inc(scalar.dma_start(
                            out=outT[dc, :, tok0 + t0:tok0 + t0 + 1024],
                            in_=yo_sb[:]), "waq", 16)

                dn.close()

            ff.close()
            phb.close()

        @block.sync
        def _(sync):
            P.wait(sync, "waq")
            P.wait(sync, "sy")

    return nc



def _host_prepare(inputs):
    x = np.asarray(inputs["L0_attn_x"], dtype=np.float32)
    W_Q = np.asarray(inputs["L0_attn_W_Q"], dtype=np.float32)
    W_K = np.asarray(inputs["L0_attn_W_K"], dtype=np.float32)
    W_V = np.asarray(inputs["L0_attn_W_V"], dtype=np.float32)
    W_O = np.asarray(inputs["L0_attn_W_O"], dtype=np.float32)

    def qkvT(W):
        return np.ascontiguousarray(
            W.transpose(0, 2, 1).reshape(NH, NCH, 128, DH).transpose(0, 2, 1, 3))

    wqT = qkvT(W_Q)
    wkT = qkvT(W_K)
    wvT = qkvT(W_V)
    woT = np.ascontiguousarray(
        W_O.reshape(NH, NCH, 128, DH).transpose(1, 3, 0, 2))

    consts = dict(
        b80=_build_B80(),
        ident=np.eye(128, dtype=np.float32).astype(bfdt),
        mtri=np.triu(np.full((128, 128), -1e9, dtype=np.float32), k=1),
        ones1=np.ones((128, 1), dtype=np.float32).astype(bfdt),
        ones1f=np.ones((1, 128), dtype=np.float32),
    )
    rec = {}
    for w, (O, I), cpname in (("g", (INTER, D), "L0_ffn_gate_cp"),
                              ("u", (INTER, D), "L0_ffn_up_cp"),
                              ("d", (D, INTER), "L0_ffn_down_cp")):
        cp = np.asarray(inputs[cpname], dtype=np.float32)
        tapidx, upow, mask = _get_recon(O, I)
        taps = np.zeros(tapidx.shape[:-1] + (8,), dtype=np.float32)
        taps[..., 0:5] = cp[tapidx]
        rec[f"taps_{w}"] = taps
        rec[f"upow_{w}"] = upow
        rec[f"mask_{w}"] = mask

    per_core = []
    for c in range(NCORES):
        xs = x[BPC * c:BPC * (c + 1)]
        xTc = np.ascontiguousarray(
            xs.transpose(2, 0, 1).reshape(D, TOK)).reshape(NCH, 128, TOK)
        im = dict(xT=xTc, wqT=wqT, wkT=wkT, wvT=wvT, woT=woT, **consts, **rec)
        per_core.append(im)
    return per_core


def kernel(**inputs):
    per_core = _host_prepare(inputs)
    nc = build_program()
    res = run_bass_kernel_spmd(nc, per_core, core_ids=list(range(NCORES)))
    out = np.empty((NBATCH, SEQ, D), dtype=np.float32)
    for c in range(NCORES):
        oT = res.results[c]["outT"].reshape(D, BPC, SEQ)
        out[BPC * c:BPC * (c + 1)] = oT.transpose(1, 2, 0)
    return out
